# version 07
# baseline (speedup 1.0000x reference)
import numpy as np
import concourse.bass as bass
import concourse.mybir as mybir
from concourse.bass_utils import run_bass_kernel_spmd
from contextlib import ExitStack

F32 = mybir.dt.float32
BF16 = mybir.dt.bfloat16
I32 = mybir.dt.int32
AF = mybir.ActivationFunctionType
ALU = mybir.AluOpType
AX = mybir.AxisListType.X

MAGIC = float(np.float32(1.5 * 2 ** 23))
D = 2048
T = 2048
H = 16
DH = 128
TR = 1024
EPS_LN = 1e-5
EPS_A = 1e-6
DEBUG = False
DEBUG2 = False
BUILD_PARTS = {"p0", "kvq", "gate", "beta", "eta", "gw", "scan", "p3", "p5"}


def build_program():
    nc = bass.Bass("TRN2", target_bir_lowering=False, debug=False)
    st = ExitStack()

    def din(name, shape, dt=F32):
        return nc.dram_tensor(name, shape, dt, kind="ExternalInput").ap()

    x_in = din("x", [T, D])
    meta_in = din("meta", [1, 64], I32)
    wq_d = din("wq", [D, D]); wk_d = din("wk", [D, D]); wv_d = din("wv", [D, D])
    wo_d = din("wo", [D, D])
    wbeta_d = din("wbeta", [D, H])
    wg1_d = din("wg1", [D, DH]); wg2_d = din("wg2", [DH, D])
    lnwG_d = din("lnwG", [128, 512]); lnbG_d = din("lnbG", [128, 512])
    lnwT_d = din("lnwT", [128, H])
    ident_d = din("ident", [128, 128])
    maskT_d = din("maskT", [128, 128])
    wrm_d = din("wrm", [128, D]); brm_d = din("brm", [128, D]); nwrm_d = din("nwrm", [128, D])
    gbbc_d = din("gbbc", [128, 4 * D]); vbbc_d = din("vbbc", [128, 4 * D])
    ob_d = din("obm", [128, 16])
    gwT_d = din("gwT", [D, 4 * D]); vwT_d = din("vwT", [D, 4 * D])
    owT_d = din("owT", [4 * D, D])

    out_d = nc.dram_tensor("out", [TR, D], F32, kind="ExternalOutput").ap()

    dbg = {}
    def dout(name, shape, dt=F32):
        dbg[name] = nc.dram_tensor(name, shape, dt, kind="ExternalOutput").ap()
        return dbg[name]

    betaT_dram = nc.dram_tensor("betaT_dram", [H, T + 16], F32).ap()
    nxT_dram = nc.dram_tensor("nxT_dram", [2, 128, 16 * TR], BF16).ap()
    h_dram = nc.dram_tensor("h_dram", [8, 128, 8192], F32).ap()
    nhT_dram = nc.dram_tensor("nhT_dram", [64, 128, 1024], BF16).ap()
    terno_dram = nc.dram_tensor("terno_dram", [16, 64, 128, 128], BF16).ap()
    parcel_dram = nc.dram_tensor("parcel_dram", [16, 128, 12288], BF16).ap()
    sxg_dram = nc.dram_tensor("sxg_dram", [2, 128, 16], F32).ap()
    opre_dram = nc.dram_tensor("opre_dram", [8, 128, D], BF16).ap()
    vtstage_dram = nc.dram_tensor("vtstage_dram", [128, 4096], BF16).ap()
    xmir_dram = nc.dram_tensor("xmir_dram", [8, 128, 2 * D], F32).ap()
    knG_dram = nc.dram_tensor("knG_dram", [128, 128, 512], BF16).ap()
    vG_dram = nc.dram_tensor("vG_dram", [128, 128, 512], BF16).ap()

    def sbuf(name, shape, dt):
        return st.enter_context(nc.sbuf_tensor(name, shape, dt))

    def psum(name, shape, dt=F32):
        return st.enter_context(nc.psum_tensor(name, shape, dt))


    etaG = sbuf("etaG_sb", [128, 512], F32)
    WG = sbuf("WG", [128, 512], F32); BG = sbuf("BG", [128, 512], F32)
    W2G = sbuf("W2G", [128, 512], F32); WBG = sbuf("WBG", [128, 512], F32)
    identb = sbuf("identb", [128, 128], BF16)
    identf = sbuf("identf", [128, 128], F32)
    maskTb = sbuf("maskTb", [128, 128], BF16)
    meta_sb = sbuf("meta_sb", [1, 64], I32)
    lnwT = sbuf("lnwT_sb", [128, H], F32)
    ob_sb = sbuf("ob_sb", [128, 16], F32)
    gx_col = sbuf("gx_col", [128, 16], F32)
    sx_col = sbuf("sx_col", [128, 16], F32)
    sxg_col = sbuf("sxg_col", [128, 32], F32)
    gwsum = sbuf("gwsum", [128, 384], F32)
    gamw128 = sbuf("gamw128", [128, 4], F32)
    smallA = sbuf("smallA", [128, 64], F32)
    onecol = sbuf("onecol", [128, 1], F32)
    cbuf = sbuf("cbuf", [128, 8], F32)
    rowA = sbuf("rowA", [1, 512], F32)
    onerow = sbuf("onerow", [1, 128], F32)
    rstdrow = sbuf("rstdrow", [1, 512], F32)
    shadow = sbuf("shadow", [128, H * 128], BF16)
    smallB = sbuf("smallB", [128, 64], F32)
    sxhrowF = sbuf("sxhrowF", [1, 1024], F32)

    ARF = sbuf("ARF", [128, 16384], F32)
    ARB = sbuf("ARB", [128, 45056], BF16)
    ibuf = sbuf("ibuf", [128, 2048], I32)[:, :]

    xrow = [ARF[:, 0:2048], ARF[:, 2048:4096]]
    xnf = ARF[:, 4096:6144]
    nxtmp = ARF[:, 6144:8192]
    sqbuf = ARF[:, 8192:10240]
    etasrc = ARF[:, 10240:10752]
    betaT = ARF[0:16, 10752:12800]
    wfbuf = [ARF[:, 12800:13824], ARF[:, 13824:14848]]
    xnT = ARB[:, 0:16384]
    xnb = ARB[:, 16384:18432]
    nxrm = ARB[:, 18432:20480]
    nxTsb = ARB[:, 20480:22528]
    wkh = [ARB[:, 22528:24576], ARB[:, 24576:26624]]
    t1T = ARB[:, 26624:28672]
    wbeta_sb = ARB[:, 28672:28928]
    wg2_sb = ARB[:, 28928:30976]
    stage = ARB[:, 30976:33024]
    rmstage = [ARB[:, 33024:34048], ARB[:, 34048:35072]]
    vstage = ARB[:, 35072:37120]
    ps_s = psum("ps_s", [128, 512])
    stA = ExitStack()

    def psumA(name, shape, dt=F32):
        return stA.enter_context(nc.psum_tensor(name, shape, dt))

    ps_t = [psumA("ps_t0", [128, 512], BF16), psumA("ps_t1", [128, 512], BF16)]
    ps_m = [psumA("ps_m0", [128, 512]), psumA("ps_m1", [128, 512])]
    ps_n = psumA("ps_n", [128, 512])

    class Sem:
        def __init__(self, name):
            self.h = st.enter_context(nc.semaphore(name))
            self.n = 0

    d1 = Sem("d1"); d2 = Sem("d2"); d3 = Sem("d3")
    tS = Sem("tS"); vS = Sem("vS"); aS = Sem("aS"); pS = Sem("pS")

    def fence(eng, sem, inst):
        inst.then_inc(sem.h, 1)
        sem.n += 1
        eng.wait_ge(sem.h, sem.n)

    blk = st.enter_context(nc.Block())

    def SY(fn):
        @blk.sync
        def _(e):
            fn(e)

    def GP(fn):
        @blk.gpsimd
        def _(e):
            fn(e)

    def TE(fn):
        @blk.tensor
        def _(e):
            fn(e)

    def VE(fn):
        @blk.vector
        def _(e):
            fn(e)

    def SC(fn):
        @blk.scalar
        def _(e):
            fn(e)

    def dma(q, sem, out, in_, **kw):
        q.dma_start(out=out, in_=in_, **kw).then_inc(sem.h, 16)
        sem.n += 16
        return sem.n

    xnT3 = xnT[:, :].rearrange("p (dt c) -> p dt c", dt=16)

    strip_sched = []
    for mi, md in enumerate((gwT_d, vwT_d, owT_d) if "gw" in BUILD_PARTS else ()):
        rows, cols = md.shape
        nch = (rows * cols) // (128 * 1024)
        cpr = cols // 1024
        for s_i in range(nch):
            strip_sched.append((mi, md, (s_i // cpr) * 128, (s_i % cpr) * 1024))
    gw_done_a = {}

    gwbufs = [wfbuf[0], wfbuf[1], ARF[:, 14848:15872]]

    def emit_gw_strip(idx):
        mi, md, r0, c0 = strip_sched[idx]
        buf = gwbufs[idx % 3]

        def gw_load(s, buf=buf, md=md, r0=r0, c0=c0, idx=idx):
            if idx - 3 in gw_done_a:
                s.wait_ge(vS.h, gw_done_a[idx - 3])
            dma(s, d3, buf[:, :], md[r0:r0 + 128, c0:c0 + 1024])
        SC(gw_load)
        w_rdy = d3.n

        def gw_acc(v, buf=buf, idx=idx, w_rdy=w_rdy):
            v.wait_ge(d3.h, w_rdy)
            v.wait_ge(vS.h, consts_v)
            v.reduce_sum(gwsum[:, idx:idx + 1], buf[:, :], axis=AX,
                         apply_absolute_value=True).then_inc(vS.h, 1)
        VE(gw_acc); vS.n += 1
        gw_done_a[idx] = vS.n


    gw_state = {"emitted": 0}


    def p0_consts(sy):
        dma(sy, d1, identf[:, :], ident_d[:, :])
        dma(sy, d1, WG[:, :], lnwG_d[:, :])
        dma(sy, d1, BG[:, :], lnbG_d[:, :])
        dma(sy, d1, lnwT[:, :], lnwT_d[:, :])
        dma(sy, d1, ob_sb[:, :], ob_d[:, :])
    SY(p0_consts)
    consts_done = d1.n

    def p0c_v(v):
        v.wait_ge(d1.h, consts_done)
        v.memset(onecol[:, :], 1.0)
        v.memset(onerow[:, :], 1.0)
        v.memset(cbuf[:, 0:1], EPS_LN)
        v.memset(cbuf[:, 1:2], EPS_A)
        v.memset(cbuf[:, 2:3], 0.0)
        v.memset(shadow[:, :], 0.0)
        v.memset(smallB[:, :], 0.0)
        v.memset(smallA[:, :], 0.0)
        v.memset(gwsum[:, :], 0.0)
        v.tensor_copy(identb[:, :], identf[:, :])
        v.tensor_tensor(W2G[:, :], WG[:, :], WG[:, :], ALU.mult)
        v.tensor_tensor(WBG[:, :], WG[:, :], BG[:, :],
                        ALU.mult).then_inc(vS.h, 1)
    VE(p0c_v); vS.n += 1
    consts_v = vS.n

    def p0_xmir(g):
        for t in range(16):
            c0 = 2048 * (t % 2)
            dma(g, d3, xmir_dram[t // 2, :, c0:c0 + 2048],
                x_in[128 * t:128 * (t + 1), :].rearrange("(o p) c -> o p c", o=1)[0])
    GP(p0_xmir)
    xmir_d = d3.n

    def p0_wload(g):
        dma(g, d2, meta_sb[:, :], meta_in[:, :])
        dma(g, d2, maskTb[:, :], maskT_d[:, :])
        dma(g, d2, wbeta_sb[:, :].rearrange("p (dt h) -> p dt h", dt=16),
            wbeta_d.rearrange("(dt p) h -> p dt h", p=128))
        dma(g, d2, wg2_sb[:, :], wg2_d[:, :])
    GP(p0_wload)
    wsmall_rdy = d2.n


    state = dict(a_xn={}, v_nx={}, t_xT={}, a_xT={}, t_nT={}, a_nT={},
                 bank_a={0: 0, 1: 0},
                 mm_bank_a={0: 0, 1: 0},
                 te_w={},
                 )

    def emit_half(hidx):
        tok0 = TR * hidx
        tile0 = 8 * hidx

        for ti in range(8):
            t = tile0 + ti
            xb = xrow[t % 2]

            def p0_dma(sy, t=t, xb=xb):
                if state["a_xn"].get(t - 2):
                    sy.wait_ge(aS.h, state["a_xn"][t - 2])
                if state["v_nx"].get(t - 2):
                    sy.wait_ge(vS.h, state["v_nx"][t - 2])
                dma(sy, d1, xb[:, :], x_in[128 * t:128 * (t + 1), :])
            SY(p0_dma)
            x_rdy = d1.n

            def p0_v1(v, xb=xb, x_rdy=x_rdy):
                v.wait_ge(d1.h, x_rdy)
                v.reduce_sum(smallA[:, 0:1], xb[:, :], axis=AX).then_inc(vS.h, 1)
            VE(p0_v1); vS.n += 1
            v_sum = vS.n

            def p0_a1(s, xb=xb, x_rdy=x_rdy, t=t):
                s.wait_ge(d1.h, x_rdy)
                s.wait_ge(vS.h, consts_v)
                s.wait_ge(tS.h, state.get("sqbuf_te", 0))
                s.activation(sqbuf[:, :], xb[:, :], AF.Square,
                             accum_out=smallA[:, 32 + t:33 + t]).then_inc(aS.h, 1)
            SC(p0_a1); aS.n += 1
            a_sq = aS.n

            def p0_v2(v, v_sum=v_sum, a_sq=a_sq, t=t):
                v.wait_ge(vS.h, v_sum)
                v.wait_ge(aS.h, a_sq)
                i1 = v.tensor_scalar(smallA[:, 4:5], smallA[:, 0:1], 1.0 / D, None, ALU.mult)
                i2 = v.tensor_scalar(smallA[:, 5:6], smallA[:, 32 + t:33 + t], 1.0 / D, None, ALU.mult)
                fence(v, vS, i2)
                i3 = v.tensor_tensor(smallA[:, 6:7], smallA[:, 4:5], smallA[:, 4:5], ALU.mult)
                fence(v, vS, i3)
                v.tensor_tensor(smallA[:, 7:8], smallA[:, 5:6], smallA[:, 6:7],
                                ALU.subtract).then_inc(vS.h, 1)
            VE(p0_v2); vS.n += 1
            v_var = vS.n

            def p0_a2(s, v_var=v_var):
                s.wait_ge(vS.h, v_var)
                s.activation(smallA[:, 8:9], smallA[:, 7:8], AF.Sqrt,
                             bias=cbuf[:, 0:1]).then_inc(aS.h, 1)
            SC(p0_a2); aS.n += 1
            a_sd = aS.n

            def p0_v3(v, a_sd=a_sd, xb=xb, t=t):
                v.wait_ge(aS.h, a_sd)
                if state["v_nx"].get(t - 1):
                    v.wait_ge(vS.h, state["v_nx"][t - 1])
                ir = v.reciprocal(smallA[:, 9:10], smallA[:, 8:9])
                fence(v, vS, ir)
                v.tensor_scalar(xnf[:, :], xb[:, :], smallA[:, 4:5], smallA[:, 9:10],
                                ALU.subtract, ALU.mult).then_inc(vS.h, 1)
            VE(p0_v3); vS.n += 1
            v_r = vS.n

            def p0_a3(s, xb=xb, v_r=v_r, t=t):
                s.wait_ge(vS.h, v_r)
                if state["t_xT"].get(t - 1):
                    s.wait_ge(tS.h, state["t_xT"][t - 1])
                s.copy(xnb[:, :], xnf[:, :]).then_inc(aS.h, 1)
            SC(p0_a3); aS.n += 1
            state["a_xn"][t] = aS.n
            a_xn = aS.n

            def p0_v4(v, t=t, a_xn=a_xn):
                v.wait_ge(aS.h, a_xn)
                v.reduce_max(gx_col[:, t:t + 1], xnf[:, :], axis=AX,
                             apply_absolute_value=True).then_inc(vS.h, 1)
            VE(p0_v4); vS.n += 1
            v_gx = vS.n

            def p0_v5(v, t=t, v_gx=v_gx):
                v.wait_ge(vS.h, v_gx)
                i1 = v.tensor_scalar(smallA[:, 12:13], gx_col[:, t:t + 1], 1e-5, None, ALU.max)
                fence(v, vS, i1)
                i2 = v.tensor_scalar(smallA[:, 13:14], smallA[:, 12:13], 1.0 / 127.0, None, ALU.mult)
                fence(v, vS, i2)
                v.reciprocal(sx_col[:, t:t + 1], smallA[:, 13:14]).then_inc(vS.h, 1)
            VE(p0_v5); vS.n += 1
            v_sx = vS.n

            def p0_v6(v, t=t, v_sx=v_sx):
                v.wait_ge(vS.h, v_sx)
                if state["t_nT"].get(t - 1):
                    v.wait_ge(tS.h, state["t_nT"][t - 1])
                v.tensor_scalar(nxtmp[:, :], xnf[:, :], sx_col[:, t:t + 1],
                                None, ALU.mult)
                v.tensor_copy(ibuf, nxtmp[:, :])
                v.tensor_copy(nxrm[:, :], ibuf).then_inc(vS.h, 1)
            VE(p0_v6); vS.n += 1
            state["v_nx"][t] = vS.n
            v_nx = vS.n

            for grp in range(4):
                bi = grp % 2
                bank = ps_t[bi]

                def p0_t(te, grp=grp, bank=bank, bi=bi, a_xn=a_xn):
                    te.wait_ge(aS.h, a_xn)
                    te.wait_ge(aS.h, state["bank_a"][bi])
                    for i in range(4):
                        dt = 4 * grp + i
                        inst = te.transpose(bank[:, 128 * i:128 * (i + 1)],
                                            xnb[:, 128 * dt:128 * (dt + 1)], identb[:, :])
                    inst.then_inc(tS.h, 1)
                TE(p0_t); tS.n += 1
                t_T = tS.n
                state["t_xT"][t] = tS.n

                def p0_ae(s, grp=grp, bank=bank, ti=ti, t_T=t_T):
                    s.wait_ge(tS.h, t_T)
                    outap = xnT3[:, 4 * grp:4 * grp + 4, 128 * ti:128 * (ti + 1)]
                    s.copy(outap, bank[:, :].rearrange("p (i c) -> p i c", i=4)).then_inc(aS.h, 1)
                SC(p0_ae); aS.n += 1
                state["bank_a"][bi] = aS.n
                state["a_xT"][t] = aS.n

            for grp in range(4):
                bi = grp % 2
                bank = ps_t[bi]

                def p0_tn(te, grp=grp, bank=bank, bi=bi, v_nx=v_nx):
                    te.wait_ge(vS.h, v_nx)
                    te.wait_ge(aS.h, state["bank_a"][bi])
                    for i in range(4):
                        dt = 4 * grp + i
                        inst = te.transpose(bank[:, 128 * i:128 * (i + 1)],
                                            nxrm[:, 128 * dt:128 * (dt + 1)], identb[:, :])
                    inst.then_inc(tS.h, 1)
                TE(p0_tn); tS.n += 1
                t_T = tS.n
                state["t_nT"][t] = tS.n

                def p0_ane(s, grp=grp, bank=bank, t_T=t_T):
                    s.wait_ge(tS.h, t_T)
                    outap = nxTsb[:, :].rearrange("p (i c) -> p i c", i=16)[:, 4 * grp:4 * grp + 4, :]
                    s.copy(outap, bank[:, :].rearrange("p (i c) -> p i c", i=4)).then_inc(aS.h, 1)
                SC(p0_ane); aS.n += 1
                state["bank_a"][bi] = aS.n
                state["a_nT"][t] = aS.n

            a_nT = state["a_nT"][t]

            def p0_gnx(g, t=t, ti=ti, a_nT=a_nT, tok0=tok0):
                g.wait_ge(aS.h, a_nT)
                outap = nxT_dram[t // 8, :, :].rearrange("p (dt c) -> p dt c", dt=16)[:, :, 128 * ti: 128 * (ti + 1)]
                dma(g, d2, outap, nxTsb[:, :].rearrange("p (i c) -> p i c", i=16))
            GP(p0_gnx)
            nx_dma = d2.n

            def p0_guard(s, nx_dma=nx_dma):
                s.wait_ge(d2.h, nx_dma)
            SC(p0_guard)

        xn_half_a = state["a_xT"][tile0 + 7]
        xn_half_t = state["t_xT"][tile0 + 7]

        for c in range(2 if "beta" in BUILD_PARTS else 0):
            def p1_beta_t(te, c=c):
                te.wait_ge(d2.h, wsmall_rdy)
                te.wait_ge(aS.h, xn_half_a)
                for dt in range(16):
                    inst = te.matmul(ps_s[0:16, :], wbeta_sb[:, 16 * dt:16 * (dt + 1)],
                                     xnT3[:, dt, 512 * c:512 * (c + 1)],
                                     start=(dt == 0), stop=(dt == 15))
                inst.then_inc(tS.h, 1)
            TE(p1_beta_t); tS.n += 1
            tv = tS.n

            def p1_beta_a(s, c=c, tv=tv, tok0=tok0):
                s.wait_ge(tS.h, tv)
                s.activation(betaT[:, tok0 + 512 * c: tok0 + 512 * (c + 1)],
                             ps_s[0:16, :], AF.Sigmoid).then_inc(aS.h, 1)
            SC(p1_beta_a); aS.n += 1
            av = aS.n

            def p1_beta_g(te, av=av):
                te.wait_ge(aS.h, av)
            TE(p1_beta_g)

        def head_proj(tag, w_dram, park, act_func, norm=False):
            for h in range(H):
                if "gw" in BUILD_PARTS:
                    for _ in range(4):
                        if gw_state["emitted"] < len(strip_sched):
                            emit_gw_strip(gw_state["emitted"])
                            gw_state["emitted"] += 1
                buf = wkh[h % 2]

                def wl(g, h=h, buf=buf, w_dram=w_dram, tag=tag):
                    key = (tag, h - 2)
                    if key in state["te_w"]:
                        g.wait_ge(tS.h, state["te_w"][key])
                    src = w_dram[:, 128 * h:128 * (h + 1)].rearrange("(dt p) c -> p dt c", p=128)
                    dma(g, d2, buf[:, :].rearrange("p (dt c) -> p dt c", dt=16), src)
                GP(wl)
                w_rdy = d2.n

                for c in range(2):
                    bi = c % 2
                    bank = ps_m[bi]

                    def mm(te, h=h, c=c, buf=buf, bank=bank, bi=bi, w_rdy=w_rdy):
                        te.wait_ge(d2.h, w_rdy)
                        te.wait_ge(aS.h, xn_half_a)
                        te.wait_ge(aS.h, state["mm_bank_a"][bi])
                        for dt in range(16):
                            inst = te.matmul(bank[:, :], buf[:, 128 * dt:128 * (dt + 1)],
                                             xnT3[:, dt, 512 * c:512 * (c + 1)],
                                             start=(dt == 0), stop=(dt == 15))
                        inst.then_inc(tS.h, 1)
                    TE(mm); tS.n += 1
                    state["te_w"][(tag, h)] = tS.n
                    mm_done = tS.n

                    def ev(s, c=c, bank=bank, mm_done=mm_done, park=park):
                        s.wait_ge(tS.h, mm_done)
                        s.wait_ge(tS.h, state.get("rm_t_last", 0))
                        s.wait_ge(d2.h, state.get("park_dma", 0))
                        dstb = vstage if park == "v" else stage
                        s.activation(dstb[:, 512 * c:512 * (c + 1)], bank[:, :],
                                     act_func).then_inc(aS.h, 1)
                    SC(ev); aS.n += 1
                    state["mm_bank_a"][bi] = aS.n

                if norm:
                    for c in range(2):
                        ev_a = aS.n

                        def kn_sq(s, c=c, ev_a=ev_a):
                            s.wait_ge(aS.h, ev_a)
                            s.wait_ge(tS.h, state.get("sqbuf_te", 0))
                            s.activation(sqbuf[:, 0:512], stage[:, 512 * c:512 * (c + 1)],
                                         AF.Square).then_inc(aS.h, 1)
                        SC(kn_sq); aS.n += 1
                        a_sq2 = aS.n

                        def kn_mm(te, a_sq2=a_sq2):
                            te.wait_ge(aS.h, a_sq2)
                            te.wait_ge(vS.h, state.get("kn_v", 0))
                            te.matmul(ps_n[0:1, 0:512], onecol[:, :], sqbuf[:, 0:512],
                                      start=True, stop=True).then_inc(tS.h, 1)
                        TE(kn_mm); tS.n += 1
                        t_mm = tS.n
                        state["sqbuf_te"] = t_mm

                        def kn_row_a(s, t_mm=t_mm):
                            s.wait_ge(tS.h, t_mm)
                            s.activation(rstdrow[0:1, 0:512], ps_n[0:1, 0:512], AF.Sqrt,
                                         bias=cbuf[0:1, 1:2]).then_inc(aS.h, 1)
                        SC(kn_row_a); aS.n += 1
                        a_row = aS.n

                        def kn_row_v(v, a_row=a_row):
                            v.wait_ge(aS.h, a_row)
                            v.reciprocal(rstdrow[0:1, 0:512],
                                         rstdrow[0:1, 0:512]).then_inc(vS.h, 1)
                        VE(kn_row_v); vS.n += 1
                        v_row = vS.n

                        def kn_bc(te, v_row=v_row):
                            te.wait_ge(vS.h, v_row)
                            te.wait_ge(vS.h, state.get("kn_v", 0))
                            te.matmul(ps_n[:, 0:512], onerow[0:1, :], rstdrow[0:1, 0:512],
                                      start=True, stop=True).then_inc(tS.h, 1)
                        TE(kn_bc); tS.n += 1
                        t_bc = tS.n

                        def kn_ap(v, c=c, t_bc=t_bc):
                            v.wait_ge(tS.h, t_bc)
                            v.tensor_tensor(stage[:, 512 * c:512 * (c + 1)],
                                            stage[:, 512 * c:512 * (c + 1)],
                                            ps_n[:, 0:512], ALU.mult).then_inc(vS.h, 1)
                        VE(kn_ap); vS.n += 1
                        state["kn_v"] = vS.n

                ev_done = aS.n
                kn_done = state.get("kn_v", 0)

                if park != "v":
                    def pk(g, h=h, ev_done=ev_done, kn_done=kn_done, park=park, tok0=tok0, norm=norm):
                        g.wait_ge(aS.h, ev_done)
                        if norm:
                            g.wait_ge(vS.h, kn_done)
                        B0 = 8 * (tok0 // TR)
                        coff = 0 if park == "k" else 2048
                        outap = parcel_dram.rearrange("B p f -> p B f")[:, B0:B0 + 8, coff + 128 * h:coff + 128 * (h + 1)]
                        dma(g, d2, outap, stage[:, 0:TR].rearrange("p (B c) -> p B c", B=8))
                    GP(pk)
                    state["park_dma"] = d2.n

                if park in ("k", "v"):
                    rmst = rmstage[h % 2]
                    rm_hist = state.setdefault("rm_hist", [])
                    if not norm:
                        def rm_fence(v, ev_done=ev_done):
                            v.wait_ge(aS.h, ev_done)
                            v.nop().then_inc(vS.h, 1)
                        VE(rm_fence); vS.n += 1
                        evf = vS.n
                    else:
                        evf = kn_done
                    for grp in range(2):
                        bi = grp % 2
                        bank = ps_t[bi]

                        def rm_t(te, h=h, grp=grp, bank=bank, bi=bi,
                                 ev_done=ev_done, evf=evf, park=park):
                            te.wait_ge(aS.h, ev_done)
                            te.wait_ge(vS.h, evf)
                            te.wait_ge(aS.h, state["bank_a"][bi])
                            srcb = vstage if park == "v" else stage
                            for i in range(4):
                                tb = 4 * grp + i
                                inst = te.transpose(bank[:, 128 * i:128 * (i + 1)],
                                                    srcb[:, 128 * tb:128 * (tb + 1)],
                                                    identb[:, :])
                            inst.then_inc(tS.h, 1)
                        TE(rm_t); tS.n += 1
                        t_T = tS.n
                        state["rm_t_last"] = tS.n

                        def rm_e(s, grp=grp, bank=bank, rmst=rmst, t_T=t_T):
                            s.wait_ge(tS.h, t_T)
                            if len(rm_hist) >= 2:
                                s.wait_ge(d2.h, rm_hist[-2][1])
                            outap = rmst.rearrange("p (tb e) -> p tb e", tb=8)[:, 4 * grp:4 * grp + 4, :]
                            s.copy(outap, bank[:, :].rearrange("p (i c) -> p i c", i=4)).then_inc(aS.h, 1)
                        SC(rm_e); aS.n += 1
                        state["bank_a"][bi] = aS.n
                    rme_done = aS.n

                    def rm_d(g, h=h, rmst=rmst, rme_done=rme_done, park=park, hidx=hidx):
                        g.wait_ge(aS.h, rme_done)
                        rmd = knG_dram if park == "k" else vG_dram
                        pp, gg = h // 4, h % 4
                        for tb in range(8):
                            j0 = 8 * (8 * hidx + tb)
                            outap = bass.AP(rmd.tensor,
                                            j0 * 65536 + pp * 16384 + 128 * gg,
                                            [[65536, 8], [512, 16], [1, 128]])
                            dma(g, d2, outap, rmst[:, 128 * tb:128 * (tb + 1)])
                    GP(rm_d)
                    rm_hist.append(("d2", d2.n))
                    state["rm_d1"] = d2.n

        if "kvq" in BUILD_PARTS:
            head_proj("k", wk_d, "k", AF.Identity, norm=True)
            head_proj("v", wv_d, "v", AF.Identity)
            head_proj("q", wq_d, "q", AF.Silu)
        state[("kvq_done", hidx)] = d2.n

        if "gate" not in BUILD_PARTS:
            state[("half_done", hidx)] = dict(t=tS.n, a=aS.n, v=vS.n, d2=d2.n)
            return
        def p1_wg1(g):
            key = ("q", H - 2)
            if ("q", H - 2) in state["te_w"]:
                g.wait_ge(tS.h, state["te_w"][key])
            src = wg1_d.rearrange("(dt p) c -> p dt c", p=128)
            dma(g, d2, wkh[0][:, :].rearrange("p (dt c) -> p dt c", dt=16), src)
        GP(p1_wg1)
        wg1_rdy = d2.n

        for c in range(2):
            bi = c % 2
            bank = ps_m[bi]

            def p1_t1(te, c=c, bank=bank, bi=bi, wg1_rdy=wg1_rdy):
                te.wait_ge(d2.h, wg1_rdy)
                te.wait_ge(aS.h, state["mm_bank_a"][bi])
                for dt in range(16):
                    inst = te.matmul(bank[:, :], wkh[0][:, 128 * dt:128 * (dt + 1)],
                                     xnT3[:, dt, 512 * c:512 * (c + 1)],
                                     start=(dt == 0), stop=(dt == 15))
                inst.then_inc(tS.h, 1)
            TE(p1_t1); tS.n += 1
            tv = tS.n

            def p1_t1e(s, c=c, bank=bank, tv=tv, tok0=tok0):
                s.wait_ge(tS.h, tv)
                s.copy(t1T[:, tok0 + 512 * c: tok0 + 512 * (c + 1)],
                       bank[:, :]).then_inc(aS.h, 1)
            SC(p1_t1e); aS.n += 1
            state["mm_bank_a"][bi] = aS.n
        t1_done = aS.n

        for ti in range(8):
            t = tile0 + ti

            for c in range(4):
                bi = c % 2
                bank = ps_m[bi]

                def p1_sg(te, t1_done=t1_done, tok0=tok0, ti=ti, c=c, bank=bank, bi=bi):
                    te.wait_ge(aS.h, t1_done)
                    te.wait_ge(d2.h, wsmall_rdy)
                    te.wait_ge(d2.h, state.get("sg_dma", 0))
                    te.wait_ge(aS.h, state["mm_bank_a"][bi])
                    te.matmul(bank[:, :],
                              t1T[:, tok0 + 128 * ti: tok0 + 128 * (ti + 1)],
                              wg2_sb[:, 512 * c:512 * (c + 1)],
                              start=True, stop=True).then_inc(tS.h, 1)
                TE(p1_sg); tS.n += 1
                sg_t = tS.n

                def p1_sga(s, sg_t=sg_t, c=c, bank=bank):
                    s.wait_ge(tS.h, sg_t)
                    s.wait_ge(tS.h, state.get("rm_t_last", 0))
                    s.activation(stage[:, 512 * c:512 * (c + 1)],
                                 bank[:, :], AF.Sigmoid).then_inc(aS.h, 1)
                SC(p1_sga); aS.n += 1
                state["mm_bank_a"][bi] = aS.n
            sg_a = aS.n

            def p1_sgd(g, t=t, sg_a=sg_a):
                g.wait_ge(aS.h, sg_a)
                dma(g, d2, parcel_dram[t, :, 6144:8192], stage[:, :])
            GP(p1_sgd)
            state["sg_dma"] = d2.n
        state[("half_done", hidx)] = dict(t=tS.n, a=aS.n, v=vS.n, d2=d2.n)

    if "p0" in BUILD_PARTS:
        emit_half(0)
        emit_half(1)

    EMIT_ETA = ("eta" in BUILD_PARTS) and ("beta" in BUILD_PARTS)
    beta_all_a = aS.n
    def p1_beta_dma(g):
        g.wait_ge(aS.h, aS.n)
        dma(g, d2, betaT_dram[:, 0:T], betaT[:, :])
    if EMIT_ETA:
        GP(p1_beta_dma)
    betaT_d_done = d2.n

    def p1_eta_dma(g):
        g.wait_ge(d2.h, betaT_d_done)
        for gidx in range(4):
            src = bass.AP(betaT_dram.tensor, gidx * (T + 16),
                          [[16, 128], [4 * (T + 16), 4], [1, 32]])
            dma(g, d2, etasrc[:, 128 * gidx:128 * (gidx + 1)].rearrange("p (a c) -> p a c", a=4), src)
    if EMIT_ETA:
        GP(p1_eta_dma)
    eta_src_done = d2.n

    def p1_eta_t(te):
        te.wait_ge(d2.h, eta_src_done)
        te.wait_ge(d1.h, consts_done)
        for gidx in range(4):
            inst = te.matmul(ps_m[0][:, 128 * gidx:128 * (gidx + 1)],
                             etasrc[:, 128 * gidx:128 * (gidx + 1)], identf[:, :],
                             start=True, stop=True)
        inst.then_inc(tS.h, 1)
    if EMIT_ETA:
        TE(p1_eta_t); tS.n += 1
    eta_t = tS.n

    def p1_eta_v(v):
        v.wait_ge(tS.h, eta_t)
        v.tensor_copy(etaG[:, :], ps_m[0][:, :])
        v.tensor_scalar(etaG[:, :], etaG[:, :], -2.0, None,
                        ALU.mult).then_inc(vS.h, 1)
    if EMIT_ETA:
        VE(p1_eta_v); vS.n += 1
    eta_done = vS.n

    stA.close()
    khsq = ARF[:, 0:512]
    khtmp = ARF[:, 512:1024]
    khh = ARF[:, 1024:1536]
    u2b = ARF[:, 1536:2048]
    u3b = ARF[:, 2048:2560]
    gwbuf = ARF[:, 2560:3584]
    x1b = ARF[:, 3584:4096]
    x2b = ARF[:, 4096:4608]
    x3b = ARF[:, 4608:5120]
    vtF = ARF[:, 5120:5632]
    kblk = ARB[:, 0:2048]
    krmG = [ARB[:, 2048:2560], ARB[:, 2560:3072]]
    vGb = [ARB[:, 3072:3584], ARB[:, 3584:4096]]
    tgtGb = ARB[:, 4096:4608]
    vtG = [ARB[:, 6656:7168], ARB[:, 7168:7680]]
    vtBD = ARB[:, 7680:9728]
    SCAN = "scan" in BUILD_PARTS
    if SCAN:
        S_ps = [psum(f"S_ps{g}", [128, 512]) for g in range(4)]
        khPS = psum("khPS", [128, 512])
        tgtPS = psum("tgtPS", [128, 512], BF16)
        krmPS = psum("krmPS", [128, 512], BF16)

        a_gate = aS.n
        v_gate = vS.n
        d2_gate = d2.n
        t_gate = tS.n

        def bc3(apx, n=128):
            return bass.AP(apx.tensor, apx.offset, list(apx.ap) + [[0, n]])

        khPS3 = khPS[:, :].rearrange("p (g e) -> p g e", g=4)
        gw3 = gwbuf[:, 0:512].rearrange("p (g e) -> p g e", g=4)
        khh3 = khh[:, :].rearrange("p (g e) -> p g e", g=4)
        x1b3 = x1b[:, :].rearrange("p (g e) -> p g e", g=4)
        x3b3 = x3b[:, :].rearrange("p (g e) -> p g e", g=4)
        x2b3 = x2b[:, :].rearrange("p (g e) -> p g e", g=4)
        eta3 = etaG[:, :].rearrange("p (g j) -> p g j", g=4)

        def s_clear(te):
            te.wait_ge(vS.h, consts_v)
            te.wait_ge(aS.h, a_gate)
            te.wait_ge(vS.h, v_gate)
            for g in range(4):
                inst = te.matmul(S_ps[g][0:1, 0:1], cbuf[0:1, 2:3], cbuf[0:1, 2:3],
                                 start=True, stop=True)
            inst.then_inc(tS.h, 1)
        TE(s_clear); tS.n += 1
        s_clear_t = tS.n

        def krm_zero_v_(v):
            v.wait_ge(tS.h, s_clear_t)
            v.memset(vtBD[:, :], 0.0)
            v.memset(krmG[0][:, :], 0.0)
            v.memset(krmG[1][:, :], 0.0)
            v.memset(vtG[0][:, :], 0.0)
            v.memset(vtG[1][:, :], 0.0).then_inc(vS.h, 1)
        VE(krm_zero_v_); vS.n += 1
        zb_v = vS.n
        krm_zero_t = s_clear_t

        if DEBUG:
            dout("vGearly_o", [128, 128, 512], BF16)
            def vg_snap(g):
                g.wait_ge(d2.h, state["rm_d1"])
                dma(g, d2, dbg["vGearly_o"], vG_dram[:, :, :])
            GP(vg_snap)

        def gpad_zero(g):
            g.wait_ge(vS.h, zb_v)
            for rmd in (knG_dram, vG_dram):
                for jo in range(16):
                    outap = bass.AP(rmd.tensor, jo * 8 * 65536 + 16 * 512,
                                    [[65536, 8], [16384, 4], [1, 8192]])
                    dma(g, d2, outap, vtBD[:, :])
        GP(gpad_zero)
        gpad_d = d2.n

        st_state = dict(a_sh=0, mm2=0, vt_dma={}, kblk_te=0, tgt_v=0, ck_dma=0,
                        mm2_t={}, vvt={}, gload={})

        def emit_gload(j):
            def gload(g, j=j):
                g.wait_ge(d2.h, state["rm_d1"])
                if j < 2:
                    g.wait_ge(d2.h, state[("kvq_done", 1)])
                    g.wait_ge(d2.h, gpad_d)
                else:
                    g.wait_ge(tS.h, st_state["mm2_t"][j - 2])
                    g.wait_ge(vS.h, st_state["vvt"][j - 2])
                dma(g, d2, krmG[j % 2][:, :], knG_dram[j, :, :])
                dma(g, d2, vGb[j % 2][:, :], vG_dram[j, :, :])
            GP(gload)
            st_state["gload"][j] = d2.n

        for B in range(16):
            def blk_dma(g, B=B):
                g.wait_ge(d2.h, state[("kvq_done", 1)])
                if st_state["kblk_te"]:
                    g.wait_ge(tS.h, st_state["kblk_te"])
                inap = parcel_dram.rearrange("B p f -> p B f")[:, B, 0:2048]
                dma(g, d2, kblk[:, :], inap)
            GP(blk_dma)
            blk_rdy = d2.n
            tgt_v = 0

            def ck_dump(g, B=B):
                if st_state["a_sh"]:
                    g.wait_ge(vS.h, st_state["a_sh"])
                else:
                    g.wait_ge(vS.h, consts_v)
                dma(g, d2, parcel_dram[B, :, 4096:6144], shadow[:, :])
            GP(ck_dump)
            st_state["ck_dma"] = d2.n

            for jj in range(8):
                j = 8 * B + jj
                vt_buf = vtG[j % 2]
                if j == 0:
                    emit_gload(0)
                    emit_gload(1)

                def te_a(te, jj=jj, blk_rdy=blk_rdy, j=j):
                    te.wait_ge(d2.h, blk_rdy)
                    if st_state["a_sh"]:
                        te.wait_ge(vS.h, st_state["a_sh"])
                    if j == 0:
                        te.wait_ge(tS.h, t_gate + 1)
                        te.wait_ge(d2.h, st_state["ck_dma"])
                    for g in range(4):
                        for p in range(4):
                            h = 4 * p + g
                            inst = te.matmul(khPS[32 * p:32 * p + 16, 128 * g:128 * (g + 1)],
                                             kblk[:, 128 * h + 16 * jj:128 * h + 16 * jj + 16],
                                             shadow[:, 128 * h:128 * (h + 1)],
                                             start=True, stop=True, tile_position=(0, 32 * p))
                    inst.then_inc(tS.h, 1)
                TE(te_a); tS.n += 1
                t_mm1 = tS.n
                st_state["kblk_te"] = t_mm1

                def po_tgt(g, j=j, t_mm1=t_mm1):
                    g.wait_ge(d2.h, st_state["gload"][j])
                    if st_state.get("u2_v"):
                        g.wait_ge(vS.h, st_state["u2_v"])
                    g.tensor_tensor(tgtGb[:, :], vGb[j % 2][:, :], krmG[j % 2][:, :],
                                    ALU.subtract).then_inc(pS.h, 1)
                GP(po_tgt); pS.n += 1
                p_tgt = pS.n

                def ve_a(v, t_mm1=t_mm1, j=j):
                    v.wait_ge(tS.h, t_mm1)
                    if j >= 2 and st_state["vt_dma"].get(j - 2):
                        v.wait_ge(d2.h, st_state["vt_dma"][j - 2])
                    v.tensor_copy(khtmp[:, :], khPS[:, :])
                    v.tensor_tensor(khsq[:, :], khtmp[:, :], khtmp[:, :], ALU.mult)
                    i1 = v.reduce_sum(smallB[:, 0:4],
                                      khtmp[:, :].rearrange("p (g e) -> p g e", g=4), axis=AX)
                    fence(v, vS, i1)
                    i2 = v.reduce_sum(smallB[:, 4:8], khsq[:, :].rearrange("p (g e) -> p g e", g=4), axis=AX)
                    fence(v, vS, i2)
                    v.tensor_scalar(smallB[:, 8:12], smallB[:, 0:4], 1.0 / 128.0, None, ALU.mult)
                    i3 = v.tensor_scalar(smallB[:, 12:16], smallB[:, 4:8], 1.0 / 128.0, None, ALU.mult)
                    fence(v, vS, i3)
                    i4 = v.tensor_tensor(smallB[:, 56:60], smallB[:, 8:12], smallB[:, 8:12], ALU.mult)
                    fence(v, vS, i4)
                    v.tensor_tensor(smallB[:, 16:20], smallB[:, 12:16], smallB[:, 56:60],
                                    ALU.subtract).then_inc(vS.h, 1)
                VE(ve_a); vS.n += 1
                v_var = vS.n

                def act_sd(s, v_var=v_var):
                    s.wait_ge(vS.h, v_var)
                    s.activation(smallB[:, 20:24], smallB[:, 16:20], AF.Sqrt,
                                 bias=cbuf[:, 1:2]).then_inc(aS.h, 1)
                SC(act_sd); aS.n += 1
                a_sd = aS.n

                def ve_b(v, a_sd=a_sd, j=j, p_tgt=p_tgt):
                    v.wait_ge(aS.h, a_sd)
                    ir = v.reciprocal(smallB[:, 24:28], smallB[:, 20:24])
                    fence(v, vS, ir)
                    im = v.tensor_tensor(smallB[:, 28:32], smallB[:, 8:12], smallB[:, 24:28], ALU.mult)
                    fence(v, vS, im)
                    v.tensor_tensor(khsq[:, :].rearrange("p (g e) -> p g e", g=4),
                                    khtmp[:, :].rearrange("p (g e) -> p g e", g=4),
                                    bc3(smallB[:, 24:28]), ALU.mult)
                    i5 = v.tensor_tensor(khh3, khsq[:, :].rearrange("p (g e) -> p g e", g=4),
                                         bc3(smallB[:, 28:32]), ALU.subtract)
                    i5.then_inc(vS.h, 1)
                    v.wait_ge(pS.h, p_tgt)
                    v.tensor_tensor(u2b[:, :], WG[:, :], tgtGb[:, :], ALU.mult)
                    v.tensor_tensor(u2b[:, :], WBG[:, :], u2b[:, :],
                                    ALU.subtract).then_inc(vS.h, 1)
                VE(ve_b); vS.n += 2
                v_khh = vS.n - 1
                v_u2 = vS.n
                st_state["u2_v"] = v_u2

                def po_a(g, v_khh=v_khh, v_u2=v_u2):
                    g.wait_ge(vS.h, v_u2)
                    g.tensor_tensor(u3b[:, :], W2G[:, :], khh[:, :], ALU.mult)
                    g.tensor_tensor(gwbuf[:, 0:512], u3b[:, :], u2b[:, :], ALU.add)
                    g.tensor_tensor(gwbuf[:, 512:1024], gwbuf[:, 0:512], khh[:, :],
                                    ALU.mult).then_inc(pS.h, 1)
                GP(po_a); pS.n += 1
                p_u3 = pS.n

                def ve_c(v, p_u3=p_u3, v_u2=v_u2, j=j):
                    v.wait_ge(pS.h, p_u3)
                    i6 = v.reduce_sum(smallB[:, 48:56],
                                      gwbuf[:, :].rearrange("p (g e) -> p g e", g=8), axis=AX)
                    fence(v, vS, i6)
                    i7 = v.tensor_scalar(smallB[:, 40:48], smallB[:, 48:56],
                                         1.0 / 128.0, None, ALU.mult)
                    fence(v, vS, i7)
                    ia = v.tensor_tensor(smallB[:, 32:36].rearrange("p (a b) -> p a b", b=1),
                                         smallB[:, 24:28].rearrange("p (a b) -> p a b", b=1),
                                         eta3[:, :, j:j + 1], ALU.mult)
                    fence(v, vS, ia)
                    v.tensor_tensor(x1b3, gw3, bc3(smallB[:, 40:44]),
                                    ALU.subtract).then_inc(vS.h, 1)
                VE(ve_c); vS.n += 1
                v_x3 = vS.n

                def po_b(g, v_x3=v_x3):
                    g.wait_ge(vS.h, v_x3)
                    g.tensor_tensor(x2b3, khh3, bc3(smallB[:, 44:48]), ALU.mult).then_inc(pS.h, 1)
                GP(po_b); pS.n += 1
                p_x2 = pS.n

                def ve_d(v, p_x2=p_x2, vt_buf=vt_buf):
                    v.wait_ge(pS.h, p_x2)
                    v.tensor_tensor(x3b[:, :], x1b[:, :], x2b[:, :], ALU.subtract)
                    iv = v.tensor_tensor(vtF[:, :].rearrange("p (g e) -> p g e", g=4),
                                         x3b[:, :].rearrange("p (g e) -> p g e", g=4),
                                         bc3(smallB[:, 32:36]), ALU.mult)
                    fence(v, vS, iv)
                    v.tensor_scalar(vt_buf[:, :], vtF[:, :], 1.0, None, ALU.mult)
                    vtBD4 = vtBD[:, :].rearrange("q (g pp e) -> q g pp e", g=4, pp=4)
                    for p in range(4):
                        outap = vtBD4[32 * p:32 * p + 16, :, p, :]
                        inap = vtF[32 * p:32 * p + 16, :].rearrange("q (g e) -> q g e", g=4)
                        inst = v.tensor_scalar(outap, inap, 1.0, None, ALU.mult)
                    inst.then_inc(vS.h, 1)
                VE(ve_d); vS.n += 1
                v_vt = vS.n

                def te_b(te, v_vt=v_vt, j=j):
                    te.wait_ge(vS.h, v_vt)
                    for g in range(4):
                        inst = te.matmul(S_ps[g][:, :],
                                         krmG[j % 2][:, 128 * g:128 * (g + 1)],
                                         vtBD[:, 512 * g:512 * (g + 1)],
                                         start=False, stop=False)
                    inst.then_inc(tS.h, 1)
                TE(te_b); tS.n += 1
                t_mm2 = tS.n
                st_state["mm2"] = t_mm2
                st_state["mm2_t"][j] = t_mm2
                st_state["vvt"][j] = v_vt

                def ve_sh(v, t_mm2=t_mm2, j=j):
                    v.wait_ge(tS.h, t_mm2)
                    if j % 8 == 0:
                        v.wait_ge(d2.h, st_state["ck_dma"])
                    for g in range(4):
                        inst = v.tensor_copy(shadow[:, :].rearrange("p (ph g2 e) -> p g2 ph e", ph=4, g2=4)[:, g, :, :],
                                             S_ps[g][:, :].rearrange("p (ph e) -> p ph e", ph=4))
                    inst.then_inc(vS.h, 1)
                VE(ve_sh); vS.n += 1
                st_state["a_sh"] = vS.n

                def gp_vt(g, v_vt=v_vt, j=j, vt_buf=vt_buf):
                    g.wait_ge(vS.h, v_vt)
                    c0 = 8192 + 512 * (j % 8)
                    dma(g, d2, parcel_dram[j // 8, :, c0:c0 + 512], vt_buf[:, :])
                GP(gp_vt)
                st_state["vt_dma"][j] = d2.n
                if j + 2 <= 127:
                    emit_gload(j + 2)


        scan_done = dict(t=tS.n, v=vS.n, a=aS.n, d2=d2.n)

    if "gw" in BUILD_PARTS:
        while gw_state["emitted"] < len(strip_sched):
            emit_gw_strip(gw_state["emitted"])
            gw_state["emitted"] += 1
    gw_all_v = vS.n
    n_strips = len(strip_sched)

    EMIT_GW = "gw" in BUILD_PARTS
    def gw_red(v):
        v.wait_ge(vS.h, gw_all_v)
        v.reduce_sum(smallA[:, 16:19],
                     gwsum[:, 0:n_strips].rearrange("p (m s) -> p m s", m=3),
                     axis=AX).then_inc(vS.h, 1)
    if EMIT_GW:
        VE(gw_red); vS.n += 1
    gwred_v = vS.n

    def gw_mm(te):
        te.wait_ge(vS.h, gwred_v)
        te.wait_ge(vS.h, consts_v)
        te.matmul(ps_s[0:1, 0:3], onecol[:, :], smallA[:, 16:19],
                  start=True, stop=True).then_inc(tS.h, 1)
    if EMIT_GW:
        TE(gw_mm); tS.n += 1
    gwmm_t = tS.n

    def gw_fin_v(v):
        v.wait_ge(tS.h, gwmm_t)
        ic = v.tensor_copy(rowA[0:1, 0:3], ps_s[0:1, 0:3])
        fence(v, vS, ic)
        v.tensor_scalar(rowA[0:1, 4:7], rowA[0:1, 0:3], 1.0 / (D * 4 * D), 1e-5,
                        ALU.mult, ALU.max).then_inc(vS.h, 1)
    if EMIT_GW:
        VE(gw_fin_v); vS.n += 1
    gwfin_v = vS.n

    def gw_bc(te):
        te.wait_ge(vS.h, gwfin_v)
        te.matmul(ps_s[:, 0:3], onerow[0:1, :], rowA[0:1, 4:7],
                  start=True, stop=True).then_inc(tS.h, 1)
    if EMIT_GW:
        TE(gw_bc); tS.n += 1
    gwbc_t = tS.n

    def gw_sxg(v):
        v.wait_ge(tS.h, gwbc_t)
        ic = v.tensor_copy(gamw128[:, 0:3], ps_s[:, 0:3])
        fence(v, vS, ic)
        ir = v.reciprocal(smallB[:, 32:48], sx_col[:, 0:16])
        fence(v, vS, ir)
        v.tensor_scalar(sxg_col[:, 0:16], smallB[:, 32:48], gamw128[:, 0:1], None, ALU.mult)
        v.tensor_scalar(sxg_col[:, 16:32], smallB[:, 32:48], gamw128[:, 1:2],
                        None, ALU.mult).then_inc(vS.h, 1)
    if EMIT_GW:
        VE(gw_sxg); vS.n += 1
    sxg_v = vS.n

    def gw_inv(v):
        v.wait_ge(vS.h, sxg_v)
        v.reciprocal(gamw128[:, 3:4], gamw128[:, 0:1])
        v.reciprocal(smallA[:, 20:21], gamw128[:, 1:2])
        ir = v.reciprocal(smallA[:, 22:23], gamw128[:, 2:3])
        fence(v, vS, ir)
        v.nop().then_inc(vS.h, 1)
    if EMIT_GW:
        VE(gw_inv); vS.n += 1
    gwinv_v = vS.n

    def gw_park(g):
        g.wait_ge(vS.h, sxg_v)
        for hf in range(2):
            inap = bass.AP(sxg_col[:, :].tensor, 8 * hf,
                           [sxg_col[:, :].ap[0], [16, 2], [1, 8]])
            dma(g, d2, sxg_dram[hf, :, :], inap)
    if EMIT_GW:
        GP(gw_park)


    if "p3" in BUILD_PARTS:
        obuf = ARF[:, 0:2048]
        sqm = ARF[:, 2048:4096]
        stc = ARF[:, 4096:4160]
        w_rm = ARB[:, 0:2048]
        b_rm = ARB[:, 2048:4096]
        nw_rm = ARB[:, 4096:6144]
        vtrm = ARB[:, 6144:8192]
        parcelb = ARB[:, 8192:20480]
        kblkP = parcelb[:, 0:2048]
        qblk = parcelb[:, 2048:4096]
        ckptb = parcelb[:, 4096:6144]
        sgateb = parcelb[:, 6144:8192]
        scSB = ARB[:, 20480:22528]
        qrm = ARB[:, 22528:24576]
        opre = ARB[:, 24576:26624]

        sd = scan_done
        p3s = dict(ld=0, sc_evac={}, qh_t={}, opark=0, qrm_v=0)

        def p3_consts(g):
            g.wait_ge(tS.h, sd["t"])
            g.wait_ge(aS.h, sd["a"])
            g.wait_ge(vS.h, sd["v"])
            dma(g, d2, w_rm[:, :], wrm_d[:, :])
            dma(g, d2, b_rm[:, :], brm_d[:, :])
            dma(g, d2, nw_rm[:, :], nwrm_d[:, :])
        GP(p3_consts)
        p3c_d = d2.n

        vt5 = vtstage_dram.rearrange("(pp cr) (jj g e) -> jj cr pp g e",
                                     pp=4, jj=8, g=4)

        for B in range(8):
            def p3_load(g, B=B):
                g.wait_ge(d2.h, p3c_d)
                if p3s["opark"]:
                    g.wait_ge(d2.h, p3s["opark"])
                with g.register(f"rb{B}") as rb:
                    g.reg_load(rb, meta_sb[0:1, 8 + B:9 + B])
                    Bg = g.snap(rb)
                    dma(g, d2, parcelb[:, :], parcel_dram[bass.ds(Bg, 1), :, :])
                g.wait_ge(d2.h, d2.n)
                dma(g, d2, vtstage_dram[:, :], parcelb[:, 8192:12288])
                g.wait_ge(d2.h, d2.n)
                for pp in range(4):
                    dma(g, d2, vtrm[:, 512 * pp:512 * (pp + 1)],
                        vt5[:, 0:16, pp, :, :])
            GP(p3_load)
            ld = d2.n

            for h in range(H):
                bank = S_ps[h // 4]
                col = 128 * (h % 4)

                def p3_sc(te, h=h, ld=ld):
                    te.wait_ge(d2.h, ld)
                    if p3s["sc_evac"].get(h - 1):
                        te.wait_ge(vS.h, p3s["sc_evac"][h - 1])
                    te.matmul(khPS[:, 0:128], kblkP[:, 128 * h:128 * (h + 1)],
                              qblk[:, 128 * h:128 * (h + 1)],
                              start=True, stop=True).then_inc(tS.h, 1)
                TE(p3_sc); tS.n += 1
                sc_t = tS.n

                def p3_scev(v, h=h, sc_t=sc_t):
                    v.wait_ge(tS.h, sc_t)
                    v.tensor_tensor(scSB[:, 128 * h:128 * (h + 1)], khPS[:, 0:128],
                                    maskTb[:, :], ALU.mult).then_inc(vS.h, 1)
                VE(p3_scev); vS.n += 1
                p3s["sc_evac"][h] = vS.n
                scv = vS.n

                def p3_qh(te, h=h, bank=bank, col=col, scv=scv, ld=ld, B=B):
                    te.wait_ge(vS.h, scv)
                    pp, g = h // 4, h % 4
                    te.matmul(bank[:, col:col + 128], qblk[:, 128 * h:128 * (h + 1)],
                              ckptb[:, 128 * h:128 * (h + 1)],
                              start=(h % 4 == 0), stop=False)
                    te.matmul(bank[:, col:col + 128], scSB[:, 128 * h:128 * (h + 1)],
                              vtrm[:, 512 * pp + 128 * g:512 * pp + 128 * g + 128],
                              start=False, stop=(h % 4 == 3)).then_inc(tS.h, 1)
                TE(p3_qh); tS.n += 1
                p3s["qh_t"][h] = tS.n
            qh_all = tS.n

            for bnk in range(4):
                def p3_qt(te, bnk=bnk, ld=ld):
                    te.wait_ge(d2.h, ld)
                    te.wait_ge(vS.h, p3s["qrm_v"])
                    for i in range(4):
                        h = 4 * bnk + i
                        inst = te.transpose(tgtPS[:, 128 * i:128 * (i + 1)],
                                            qblk[:, 128 * h:128 * (h + 1)], identb[:, :])
                    inst.then_inc(tS.h, 1)
                TE(p3_qt); tS.n += 1
                qt_t = tS.n

                def p3_qte(v, bnk=bnk, qt_t=qt_t):
                    v.wait_ge(tS.h, qt_t)
                    v.tensor_copy(qrm[:, 512 * bnk:512 * (bnk + 1)],
                                  tgtPS[:, :]).then_inc(vS.h, 1)
                VE(p3_qte); vS.n += 1
                p3s["qrm_v"] = vS.n
            qrm_v = vS.n

            def p3_st1(v, qh_all=qh_all):
                v.wait_ge(tS.h, qh_all)
                for bnk in range(4):
                    i1 = v.reduce_sum(stc[:, 4 * bnk:4 * bnk + 4],
                                      S_ps[bnk][:, :].rearrange("p (g e) -> p g e", g=4), axis=AX)
                fence(v, vS, i1)
                v.nop().then_inc(vS.h, 1)
            VE(p3_st1); vS.n += 1
            st1_v = vS.n

            def p3_sq(s, qh_all=qh_all):
                s.wait_ge(tS.h, qh_all)
                for bnk in range(4):
                    inst = s.activation(sqm[:, 512 * bnk:512 * (bnk + 1)],
                                        S_ps[bnk][:, :], AF.Square)
                inst.then_inc(aS.h, 1)
            SC(p3_sq); aS.n += 1
            sq_a = aS.n

            def p3_st2(v, sq_a=sq_a, st1_v=st1_v):
                v.wait_ge(aS.h, sq_a)
                v.wait_ge(vS.h, st1_v)
                i2 = v.reduce_sum(smallB[:, 16:32],
                                  sqm[:, :].rearrange("p (g e) -> p g e", g=16), axis=AX)
                fence(v, vS, i2)
                i3 = v.tensor_scalar(smallB[:, 0:16], stc[:, 0:16], 1.0 / 128.0, None, ALU.mult)
                i4 = v.tensor_scalar(smallB[:, 16:32], smallB[:, 16:32], 1.0 / 128.0, None, ALU.mult)
                fence(v, vS, i4)
                i5 = v.tensor_tensor(smallB[:, 32:48], smallB[:, 0:16], smallB[:, 0:16], ALU.mult)
                fence(v, vS, i5)
                v.tensor_tensor(smallB[:, 16:32], smallB[:, 16:32], smallB[:, 32:48],
                                ALU.subtract).then_inc(vS.h, 1)
            VE(p3_st2); vS.n += 1
            st2_v = vS.n

            def p3_sd(s, st2_v=st2_v):
                s.wait_ge(vS.h, st2_v)
                s.activation(smallB[:, 32:48], smallB[:, 16:32], AF.Sqrt,
                             bias=cbuf[:, 1:2]).then_inc(aS.h, 1)
            SC(p3_sd); aS.n += 1
            sd_a = aS.n

            def p3_ochain(v, sd_a=sd_a, qrm_v=qrm_v, ld=ld):
                v.wait_ge(aS.h, sd_a)
                v.wait_ge(vS.h, qrm_v)
                ir = v.reciprocal(smallB[:, 48:64], smallB[:, 32:48])
                fence(v, vS, ir)
                im = v.tensor_tensor(smallB[:, 0:16], smallB[:, 0:16], smallB[:, 48:64], ALU.mult)
                fence(v, vS, im)
                for bnk in range(4):
                    ob3 = obuf[:, 512 * bnk:512 * (bnk + 1)].rearrange("p (g e) -> p g e", g=4)
                    v.tensor_tensor(ob3, S_ps[bnk][:, :].rearrange("p (g e) -> p g e", g=4),
                                    bc3(smallB[:, 48 + 4 * bnk:52 + 4 * bnk]), ALU.mult)
                    iw = v.tensor_tensor(ob3, ob3, bc3(smallB[:, 4 * bnk:4 * bnk + 4]), ALU.subtract)
                fence(v, vS, iw)
                v.tensor_tensor(obuf[:, :], obuf[:, :], w_rm[:, :], ALU.mult)
                v.tensor_tensor(obuf[:, :], obuf[:, :], b_rm[:, :], ALU.add)
                v.tensor_tensor(obuf[:, :], obuf[:, :], qrm[:, :], ALU.add)
                iq = v.tensor_tensor(obuf[:, :], obuf[:, :], sgateb[:, :], ALU.mult)
                iq.then_inc(vS.h, 1)
            VE(p3_ochain); vS.n += 1
            oc_v = vS.n

            def p3_rsq(s, oc_v=oc_v):
                s.wait_ge(vS.h, oc_v)
                s.activation(sqm[:, :], obuf[:, :], AF.Square).then_inc(aS.h, 1)
            SC(p3_rsq); aS.n += 1
            rsq_a = aS.n

            def p3_rms1(v, rsq_a=rsq_a):
                v.wait_ge(aS.h, rsq_a)
                i1 = v.reduce_sum(smallB[:, 0:16],
                                  sqm[:, :].rearrange("p (g e) -> p g e", g=16), axis=AX)
                fence(v, vS, i1)
                i2 = v.tensor_scalar(smallB[:, 0:16], smallB[:, 0:16], 1.0 / 128.0, None, ALU.mult)
                fence(v, vS, i2)
                v.nop().then_inc(vS.h, 1)
            VE(p3_rms1); vS.n += 1
            rms1_v = vS.n

            def p3_rsd(s, rms1_v=rms1_v):
                s.wait_ge(vS.h, rms1_v)
                s.activation(smallB[:, 16:32], smallB[:, 0:16], AF.Sqrt,
                             bias=cbuf[:, 1:2]).then_inc(aS.h, 1)
            SC(p3_rsd); aS.n += 1
            rsd_a = aS.n

            def p3_fin(v, rsd_a=rsd_a):
                v.wait_ge(aS.h, rsd_a)
                ir = v.reciprocal(smallB[:, 16:32], smallB[:, 16:32])
                fence(v, vS, ir)
                ob3 = obuf[:, :].rearrange("p (g e) -> p g e", g=16)
                v.tensor_tensor(ob3, ob3, bc3(smallB[:, 16:32]), ALU.mult)
                v.tensor_tensor(opre[:, :], obuf[:, :], nw_rm[:, :],
                                ALU.mult).then_inc(vS.h, 1)
            VE(p3_fin); vS.n += 1
            fin_v = vS.n

            def p3_park(g, B=B, fin_v=fin_v):
                g.wait_ge(vS.h, fin_v)
                dma(g, d2, opre_dram[B, :, :], opre[:, :])
            GP(p3_park)
            p3s["opark"] = d2.n
        p3_done = dict(t=tS.n, v=vS.n, a=aS.n, d2=d2.n)

    if "p5" in BUILD_PARTS:
        if DEBUG2:
            dout("h_o2", [8, 128, 8192])
            dout("nhT_o2", [64, 128, 1024], BF16)
            dout("p1f_o2", [16, 128, 512])
            dout("gb_o2", [16, 128, 512], BF16)
        if DEBUG:
            dout("sg3_o", [4, 16, 128, 256])
            dout("gg3_o", [4, 16, 128, 256], BF16)
            dout("ot3_o", [4, 16, 128, 256])
            dout("tern_o", [2, 128, 8192], BF16)
            dout("ternodt_o", [128, 8192], BF16)
            dout("nhq_o", [128, 16384], BF16)
            dout("sxbc_o", [128, 256])
            dout("p1f_o", [8, 128, 512])
            dout("p2f_o", [8, 128, 512])
            dout("sxgc_o", [128, 32])
        pd = p3_done
        wfst = ARF[:, 0:8192]
        gbbc = ARF[:, 8192:8704]
        vbbc = ARF[:, 8704:9216]
        t1fb = [ARF[:, 9216:9728], ARF[:, 10240:10752]]
        t2fb = [ARF[:, 9728:10240], ARF[:, 10752:11264]]
        htile = ARF[:, 8192:16384]
        sgf = ARF[:, 8192:8448]
        sxbc = ARF[:, 8448:8704]
        xstage = [ARF[:, 9216:11264], ARF[:, 11264:13312]]
        otmp = ARF[:, 13312:13568]
        nxTrb = ARB[:, 0:16384]
        terng = ARB[:, 16384:24576]
        ternv = ARB[:, 24576:32768]
        gateb = ARB[:, 32768:33280]
        nh = ARB[:, 33280:41472]
        nhT = ARB[:, 41472:43520]
        nhq = ARB[:, 0:16384]
        ternodt = [ARB[:, 16384:24576], ARB[:, 35328:43520]]
        opreTq = ARB[:, 24576:28672]
        wobuf = ARB[:, 28672:30720]
        gslice = ARB[:, 30720:30976]
        opst = ARB[:, 33280:35328]
        x3 = xmir_dram

        def p5_init(g):
            g.wait_ge(d2.h, pd["d2"])
            g.wait_ge(vS.h, max(gwinv_v, pd["v"]))
            g.wait_ge(aS.h, pd["a"])
            g.wait_ge(tS.h, pd["t"])
            with g.register("rp5a") as rhf:
                g.reg_load(rhf, meta_sb[0:1, 3:4])
                hsel = g.snap(rhf)
                dma(g, d2, nxTrb[:, :], nxT_dram[bass.ds(hsel, 1), :, :])
            with g.register("rp5b") as rhf2:
                g.reg_load(rhf2, meta_sb[0:1, 3:4])
                hsel2 = g.snap(rhf2)
                dma(g, d2, sxg_col[:, 0:16], sxg_dram[bass.ds(hsel2, 1), :, :])
        GP(p5_init)
        p5i_d = d2.n

        def p5_initv(v):
            v.wait_ge(d2.h, p5i_d)
            v.nop().then_inc(vS.h, 1)
        VE(p5_initv); vS.n += 1
        p5i_v = vS.n

        l1 = dict(tern_v=p5i_v, mm_t={}, evv={}, hv={}, hdma={})
        for hc in range(16):
            for mi, (wd, tern, invcol) in enumerate(
                    ((gwT_d, terng, gamw128[:, 3:4]), (vwT_d, ternv, smallA[:, 20:21]))):
                def l1_wl(g, hc=hc, wd=wd):
                    g.wait_ge(vS.h, l1["tern_v"])
                    dma(g, d3, wfst[:, :].rearrange("p (dt c) -> p dt c", dt=16),
                        wd[:, 512 * hc:512 * (hc + 1)].rearrange("(dt p) c -> p dt c", p=128))
                GP(l1_wl)
                wrdy = d3.n

                def l1_tern(v, tern=tern, invcol=invcol, wrdy=wrdy, hc=hc):
                    v.wait_ge(d3.h, wrdy)
                    if (hc - 1, "mm") in l1["mm_t"]:
                        v.wait_ge(tS.h, l1["mm_t"][(hc - 1, "mm")])
                    v.tensor_scalar(wfst[:, :], wfst[:, :], invcol, None, ALU.mult)
                    for s4 in range(4):
                        v.tensor_copy(ibuf, wfst[:, 2048 * s4:2048 * (s4 + 1)])
                        inst = v.tensor_scalar(tern[:, 2048 * s4:2048 * (s4 + 1)], ibuf,
                                               -1.0, 1.0, ALU.max, ALU.min)
                    inst.then_inc(vS.h, 1)
                VE(l1_tern); vS.n += 1
                l1["tern_v"] = vS.n
                if DEBUG and hc == 0:
                    def l1_terndbg(sy, mi=mi, tv=l1["tern_v"], tern=tern):
                        sy.wait_ge(vS.h, tv)
                        dma(sy, d1, dbg["tern_o"][mi, :, :], tern[:, :])
                        if mi == 0:
                            dma(sy, d1, dbg["sxgc_o"], sxg_col[:, :])
                    SY(l1_terndbg)
            tern_done = vS.n

            def l1_bias(sy, hc=hc):
                if (hc - 1, 7) in l1["evv"]:
                    sy.wait_ge(vS.h, l1["evv"][(hc - 1, 7)])
                else:
                    sy.wait_ge(vS.h, p5i_v)
                dma(sy, d1, gbbc[:, :], gbbc_d[:, 512 * hc:512 * (hc + 1)])
                dma(sy, d1, vbbc[:, :], vbbc_d[:, 512 * hc:512 * (hc + 1)])
            SY(l1_bias)
            bias_rdy = d1.n

            for ti in range(8):
                def l1_mm(te, hc=hc, ti=ti, tern_done=tern_done):
                    te.wait_ge(vS.h, tern_done)
                    te.wait_ge(d2.h, p5i_d)
                    prev = l1["evv"].get((hc, ti - 1)) or l1["evv"].get((hc - 1, 7))
                    if prev:
                        te.wait_ge(vS.h, prev)
                    for dt in range(16):
                        te.matmul(S_ps[0][:, :], nxTrb[:, 1024 * dt + 128 * ti:1024 * dt + 128 * (ti + 1)],
                                  terng[:, 512 * dt:512 * (dt + 1)],
                                  start=(dt == 0), stop=(dt == 15))
                    for dt in range(16):
                        inst = te.matmul(S_ps[1][:, :], nxTrb[:, 1024 * dt + 128 * ti:1024 * dt + 128 * (ti + 1)],
                                         ternv[:, 512 * dt:512 * (dt + 1)],
                                         start=(dt == 0), stop=(dt == 15))
                    inst.then_inc(tS.h, 1)
                TE(l1_mm); tS.n += 1
                l1["mm_t"][(hc, "mm")] = tS.n
                mmt = tS.n

                def l1_ev(v, hc=hc, ti=ti, mmt=mmt, bias_rdy=bias_rdy):
                    v.wait_ge(tS.h, mmt)
                    v.wait_ge(d1.h, bias_rdy)
                    gi = (8 * hc + ti) % 2
                    prevd = l1["hdma"].get((hc, ti - 2)) or l1["hdma"].get((hc - 1, 6 + ti))
                    if prevd:
                        v.wait_ge(d2.h, prevd)
                    t1f, t2f = t1fb[gi], t2fb[gi]
                    v.tensor_scalar(t1f[:, :], S_ps[0][:, :], sxg_col[:, ti:ti + 1],
                                    None, ALU.mult)
                    v.tensor_tensor(t1f[:, :], t1f[:, :], gbbc[:, :], ALU.add)
                    v.tensor_scalar(t2f[:, :], S_ps[1][:, :], sxg_col[:, 8 + ti:9 + ti],
                                    None, ALU.mult)
                    v.tensor_tensor(t2f[:, :], t2f[:, :], vbbc[:, :],
                                    ALU.add).then_inc(vS.h, 1)
                VE(l1_ev); vS.n += 1
                l1["evv"][(hc, ti)] = vS.n
                evv = vS.n
                if DEBUG and hc == 0:
                    def l1_evdbg(sy, ti=ti, evv=evv, hc=hc):
                        sy.wait_ge(vS.h, evv)
                        gi = (8 * hc + ti) % 2
                        dma(sy, d1, dbg["p1f_o"][ti, :, :], t1fb[gi][:, :])
                        dma(sy, d1, dbg["p2f_o"][ti, :, :], t2fb[gi][:, :])
                    SY(l1_evdbg)
                    l1["evdbg"] = d1.n

                def l1_silu(sc, evv=evv, hc=hc, ti=ti):
                    sc.wait_ge(vS.h, evv)
                    if DEBUG and hc == 0:
                        sc.wait_ge(d1.h, l1["evdbg"])
                    sc.activation(gateb[:, :], t1fb[(8 * hc + ti) % 2][:, :],
                                  AF.Silu).then_inc(aS.h, 1)
                SC(l1_silu); aS.n += 1
                sla = aS.n

                def l1_h(v, sla=sla, hc=hc, ti=ti):
                    v.wait_ge(aS.h, sla)
                    gi = (8 * hc + ti) % 2
                    v.tensor_tensor(t1fb[gi][:, :], gateb[:, :], t2fb[gi][:, :],
                                    ALU.mult).then_inc(vS.h, 1)
                VE(l1_h); vS.n += 1
                l1["hv"][(hc, ti)] = vS.n
                hv = vS.n
                if DEBUG2 and ti == 0:
                    def l1_ff(sy, hc=hc, evv=evv, sla=sla):
                        sy.wait_ge(vS.h, evv)
                        dma(sy, d1, dbg["p1f_o2"][hc, :, :], t1fb[(8 * hc) % 2][:, :])
                        sy.wait_ge(aS.h, sla)
                        dma(sy, d1, dbg["gb_o2"][hc, :, :], gateb[:, :])
                    SY(l1_ff)

                def l1_hd(g, hc=hc, ti=ti, hv=hv):
                    g.wait_ge(vS.h, hv)
                    dma(g, d2, h_dram[ti, :, 512 * hc:512 * (hc + 1)],
                        t1fb[(8 * hc + ti) % 2][:, :])
                GP(l1_hd)
                l1["hdma"][(hc, ti)] = d2.n
        l1_done = dict(v=vS.n, t=tS.n, d2=d2.n)

        l2t = dict(v=l1["tern_v"])
        for ht in range(64):
            def o_wl(sy, ht=ht):
                sy.wait_ge(vS.h, l2t["v"])
                if ht == 0:
                    sy.wait_ge(vS.h, l1_done["v"])
                dma(sy, d3, wfst[:, 0:2048], owT_d[128 * ht:128 * (ht + 1), :])
            SC(o_wl)
            wrdy = d3.n

            def o_t1(v, wrdy=wrdy, ht=ht):
                v.wait_ge(d3.h, wrdy)
                if ht == 0:
                    v.wait_ge(vS.h, l1_done["v"])
                if ("otd", ht - 1) in l2t:
                    v.wait_ge(d2.h, l2t[("otd", ht - 1)])
                v.tensor_scalar(wfst[:, 0:2048], wfst[:, 0:2048], smallA[:, 22:23],
                                None, ALU.mult)
                v.tensor_copy(ibuf, wfst[:, 0:2048])
                v.tensor_scalar(ternv[:, 0:2048], ibuf, -1.0, 1.0,
                                ALU.max, ALU.min).then_inc(vS.h, 1)
            VE(o_t1); vS.n += 1
            l2t["v"] = vS.n
            t3v = vS.n

            def o_td(g, ht=ht, t3v=t3v):
                g.wait_ge(vS.h, t3v)
                outap = terno_dram[:, ht, :, :].rearrange("dt p c -> p dt c")
                dma(g, d2, outap, ternv[:, 0:2048].rearrange("p (dt c) -> p dt c", dt=16))
            GP(o_td)
            l2t[("otd", ht)] = d2.n
        terno_done = d2.n

        l2 = dict(nhv={}, ev={}, qd={}, last_t=0, sxe_a=0)
        for ti in range(8):
            def l2_hl(g, ti=ti):
                g.wait_ge(d2.h, l1_done["d2"])
                if (ti - 1) in l2["nhv"]:
                    g.wait_ge(vS.h, l2["nhv"][ti - 1])
                dma(g, d2, htile[:, :], h_dram[ti, :, :])
            GP(l2_hl)
            hrdy = d2.n

            def l2_g(v, hrdy=hrdy, ti=ti):
                v.wait_ge(d2.h, hrdy)
                if l2["last_t"]:
                    v.wait_ge(tS.h, l2["last_t"])
                i1 = v.reduce_max(smallB[:, 0:1], htile[:, :], axis=AX,
                                  apply_absolute_value=True)
                fence(v, vS, i1)
                i2 = v.tensor_scalar(smallB[:, 1:2], smallB[:, 0:1], 1e-5, None, ALU.max)
                fence(v, vS, i2)
                i3 = v.tensor_scalar(smallB[:, 2:3], smallB[:, 1:2], 1.0 / 127.0, None, ALU.mult)
                fence(v, vS, i3)
                ir = v.reciprocal(smallB[:, 3:4], smallB[:, 2:3])
                fence(v, vS, ir)
                i4 = v.tensor_tensor(smallB[:, 4:5], smallB[:, 2:3], gamw128[:, 2:3], ALU.mult)
                fence(v, vS, i4)
                v.tensor_scalar(htile[:, :], htile[:, :], smallB[:, 3:4], None, ALU.mult)
                for s4 in range(4):
                    v.tensor_copy(ibuf, htile[:, 2048 * s4:2048 * (s4 + 1)])
                    inst = v.tensor_copy(nh[:, 2048 * s4:2048 * (s4 + 1)], ibuf)
                inst.then_inc(vS.h, 1)
            VE(l2_g); vS.n += 1
            l2["nhv"][ti] = vS.n
            nhv = vS.n

            def l2_sxt(te, ti=ti, nhv=nhv):
                te.wait_ge(vS.h, nhv)
                if l2["sxe_a"]:
                    te.wait_ge(aS.h, l2["sxe_a"])
                te.matmul(khPS[0:1, 0:128], smallB[:, 4:5], identf[:, :],
                          start=True, stop=True).then_inc(tS.h, 1)
            TE(l2_sxt); tS.n += 1
            sxt = tS.n

            def l2_sxe(s, ti=ti, sxt=sxt):
                s.wait_ge(tS.h, sxt)
                s.copy(sxhrowF[0:1, 128 * ti:128 * (ti + 1)],
                       khPS[0:1, 0:128]).then_inc(aS.h, 1)
            SC(l2_sxe); aS.n += 1
            l2["sxe_a"] = aS.n

            for grp in range(16):
                def l2_t(te, grp=grp, nhv=nhv, ti=ti):
                    te.wait_ge(vS.h, nhv)
                    prev = l2["ev"].get((ti, grp - 1)) or l2["ev"].get((ti - 1, 15))
                    if prev:
                        te.wait_ge(vS.h, prev)
                    for i in range(4):
                        ht = 4 * grp + i
                        inst = te.transpose(tgtPS[:, 128 * i:128 * (i + 1)],
                                            nh[:, 128 * ht:128 * (ht + 1)], identb[:, :])
                    inst.then_inc(tS.h, 1)
                TE(l2_t); tS.n += 1
                l2["last_t"] = tS.n
                tt = tS.n

                def l2_te(v, grp=grp, tt=tt, ti=ti):
                    v.wait_ge(tS.h, tt)
                    if grp % 4 == 0:
                        prevq = l2["qd"].get((ti, grp // 4 - 1)) or l2["qd"].get((ti - 1, 3))
                        if prevq:
                            v.wait_ge(d2.h, prevq)
                    v.tensor_copy(nhT[:, 512 * (grp % 4):512 * (grp % 4 + 1)],
                                  tgtPS[:, :]).then_inc(vS.h, 1)
                VE(l2_te); vS.n += 1
                l2["ev"][(ti, grp)] = vS.n
                if grp % 4 == 3:
                    tev = vS.n

                    def l2_nhd(g, grp=grp, ti=ti, tev=tev):
                        g.wait_ge(vS.h, tev)
                        h0 = 16 * (grp // 4)
                        outap = nhT_dram[h0:h0 + 16, :, 128 * ti:128 * (ti + 1)].rearrange("ht p c -> p ht c")
                        dma(g, d2, outap, nhT[:, 0:2048].rearrange("p (ht c) -> p ht c", ht=16))
                    GP(l2_nhd)
                    l2["qd"][(ti, grp // 4)] = d2.n
        l2_done = dict(v=vS.n, t=tS.n, d2=d2.n, a=aS.n)

        wo3 = wo_d.rearrange("(dt p) c -> p dt c", p=128)
        l3 = dict(res={}, oTe_v=0, oT_t=0, mm3_t=0, ev3_v=0, acc_v=0, xd={})
        for tq in range(4):
            def l3_ld(g, tq=tq):
                g.wait_ge(d2.h, max(terno_done, l2_done["d2"]))
                if tq > 0:
                    g.wait_ge(vS.h, l3["res"][tq - 1])
                    g.wait_ge(d2.h, l3["xd"][tq - 1])
                g.wait_ge(d3.h, xmir_d)
                with g.register(f"rx{tq}") as rx:
                    g.reg_load(rx, meta_sb[0:1, 40 + tq:41 + tq])
                    xq0 = g.snap(rx)
                    dma(g, d2, ARF[:, 9216:13312], x3[bass.ds(xq0, 1), :, :])
                inap = nhT_dram[:, :, 256 * tq:256 * (tq + 1)].rearrange("ht p c -> p ht c")
                dma(g, d2, nhq[:, :].rearrange("p (ht c) -> p ht c", ht=64), inap)
            GP(l3_ld)
            ld3 = d2.n

            for b2 in range(2):
                def l3_og(g, tq=tq, b2=b2, ld3=ld3):
                    g.wait_ge(d2.h, max(ld3, terno_done))
                    if l3["oTe_v"]:
                        g.wait_ge(vS.h, l3["oTe_v"])
                    dma(g, d2, opst[:, :], opre_dram[2 * tq + b2, :, :])
                GP(l3_og)
                og_d = d2.n
                for grp in range(4):
                    def l3_oT(te, grp=grp, og_d=og_d):
                        te.wait_ge(d2.h, og_d)
                        if l3["oTe_v"]:
                            te.wait_ge(vS.h, l3["oTe_v"])
                        for i in range(4):
                            h = 4 * grp + i
                            inst = te.transpose(krmPS[:, 128 * i:128 * (i + 1)],
                                                opst[:, 128 * h:128 * (h + 1)],
                                                identb[:, :])
                        inst.then_inc(tS.h, 1)
                    TE(l3_oT); tS.n += 1
                    l3["oT_t"] = tS.n
                    ott = tS.n

                    def l3_oTe(v, b2=b2, grp=grp, ott=ott):
                        v.wait_ge(tS.h, ott)
                        outap = opreTq[:, :].rearrange("p (dt c) -> p dt c", dt=16)[:, 4 * grp:4 * grp + 4, 128 * b2:128 * (b2 + 1)]
                        v.tensor_copy(outap, krmPS[:, :].rearrange("p (i c) -> p i c", i=4)).then_inc(vS.h, 1)
                    VE(l3_oTe); vS.n += 1
                    l3["oTe_v"] = vS.n
            oT_done_v = vS.n

            def l3_sx(te, tq=tq, oT_done_v=oT_done_v):
                te.wait_ge(aS.h, l2_done["a"])
                te.wait_ge(vS.h, max(oT_done_v, p5i_v))
                te.matmul(ps_s[:, 0:256], onerow[0:1, :],
                          sxhrowF[0:1, 256 * tq:256 * (tq + 1)],
                          start=True, stop=True).then_inc(tS.h, 1)
            TE(l3_sx); tS.n += 1
            sxt3 = tS.n

            def l3_sxe(v, sxt3=sxt3):
                v.wait_ge(tS.h, sxt3)
                v.tensor_copy(sxbc[:, :], ps_s[:, 0:256]).then_inc(vS.h, 1)
            VE(l3_sxe); vS.n += 1
            sxe3 = vS.n

            for dtout in range(16):
                def l3_tl(g, tq=tq, dtout=dtout):
                    if DEBUG and l3.get("mmdbg"):
                        g.wait_ge(d1.h, l3["mmdbg"])
                    gi = dtout % 2
                    prev = l3.get(("mm3", tq, dtout - 2)) or l3.get(("mm3", tq - 1, 14 + dtout))
                    if prev:
                        g.wait_ge(tS.h, prev)
                    elif not l3["mm3_t"]:
                        g.wait_ge(d2.h, l2_done["d2"])
                    else:
                        g.wait_ge(tS.h, l3["mm3_t"])
                    dma(g, d3, ternodt[gi][:, :].rearrange("p (ht c) -> p ht c", ht=64),
                        terno_dram[dtout, :, :, :].rearrange("ht p c -> p ht c"))
                GP(l3_tl)
                tl3 = d3.n

                def l3_wol(g, tq=tq, dtout=dtout):
                    if l3["mm3_t"]:
                        g.wait_ge(tS.h, l3["mm3_t"])
                    dma(g, d2, wobuf[:, :].rearrange("p (dt c) -> p dt c", dt=16),
                        wo3[:, :, 128 * dtout:128 * (dtout + 1)])
                GP(l3_wol)
                wol3 = d2.n

                def l3_mm(te, tl3=tl3, wol3=wol3, ld3=ld3, oT_done_v=oT_done_v, sxe3=sxe3, dtout=dtout):
                    te.wait_ge(d3.h, tl3)
                    te.wait_ge(d2.h, max(wol3, ld3))
                    te.wait_ge(vS.h, max(oT_done_v, sxe3))
                    if l3["ev3_v"]:
                        te.wait_ge(vS.h, l3["ev3_v"])
                    for ht in range(64):
                        te.matmul(S_ps[0][:, 0:256], ternodt[dtout % 2][:, 128 * ht:128 * (ht + 1)],
                                  nhq[:, 256 * ht:256 * (ht + 1)],
                                  start=(ht == 0), stop=(ht == 63))
                    for dt in range(16):
                        inst = te.matmul(S_ps[1][:, 0:256], wobuf[:, 128 * dt:128 * (dt + 1)],
                                         opreTq[:, 256 * dt:256 * (dt + 1)],
                                         start=(dt == 0), stop=(dt == 15))
                    inst.then_inc(tS.h, 1)
                TE(l3_mm); tS.n += 1
                l3["mm3_t"] = tS.n
                l3[("mm3", tq, dtout)] = tS.n
                mm3 = tS.n
                if DEBUG and tq == 0 and dtout == 0:
                    def l3_mmdbg(sy, mm3=mm3):
                        sy.wait_ge(tS.h, mm3)
                        dma(sy, d1, dbg["ternodt_o"], ternodt[0][:, :])
                        dma(sy, d1, dbg["nhq_o"], nhq[:, :])
                        dma(sy, d1, dbg["sxbc_o"], sxbc[:, :])
                    SY(l3_mmdbg)
                    l3["mmdbg"] = d1.n

                def l3_g(v, mm3=mm3, dtout=dtout):
                    v.wait_ge(tS.h, mm3)
                    if DEBUG and l3.get("dbg_d"):
                        v.wait_ge(d1.h, l3["dbg_d"])
                    v.tensor_tensor(otmp[:, :], S_ps[0][:, 0:256], sxbc[:, :],
                                    ALU.mult)
                    v.tensor_scalar(otmp[:, :], otmp[:, :], ob_sb[:, dtout:dtout + 1],
                                    None, ALU.add).then_inc(vS.h, 1)
                VE(l3_g); vS.n += 1
                gv = vS.n

                def l3_silu(sc, gv=gv):
                    sc.wait_ge(vS.h, gv)
                    sc.activation(gslice[:, :], otmp[:, :],
                                  AF.Silu).then_inc(aS.h, 1)
                SC(l3_silu); aS.n += 1
                sil3 = aS.n

                def l3_sg(v, sil3=sil3):
                    v.wait_ge(aS.h, sil3)
                    v.tensor_tensor(sgf[:, :], S_ps[1][:, 0:256], gslice[:, :],
                                    ALU.mult).then_inc(vS.h, 1)
                VE(l3_sg); vS.n += 1
                sgv = vS.n

                if DEBUG:
                    def l3_dbg(sy, tq=tq, dtout=dtout, sgv=sgv):
                        sy.wait_ge(vS.h, sgv)
                        dma(sy, d1, dbg["sg3_o"][tq, dtout, :, :], sgf[:, :])
                        dma(sy, d1, dbg["gg3_o"][tq, dtout, :, :], gslice[:, :])
                        dma(sy, d1, dbg["ot3_o"][tq, dtout, :, :], otmp[:, :])
                    SY(l3_dbg)
                    l3["dbg_d"] = d1.n

                for g2 in range(2):
                    def l3_T(te, g2=g2, sgv=sgv, ld3=ld3):
                        te.wait_ge(vS.h, sgv)
                        te.wait_ge(d2.h, ld3)
                        if l3["acc_v"]:
                            te.wait_ge(vS.h, l3["acc_v"])
                        te.matmul(khPS[:, 0:128], sgf[:, 128 * g2:128 * (g2 + 1)],
                                  identf[:, :], start=True, stop=True).then_inc(tS.h, 1)
                    TE(l3_T); tS.n += 1
                    t3 = tS.n

                    def l3_acc(v, g2=g2, dtout=dtout, t3=t3, tq=tq):
                        v.wait_ge(tS.h, t3)
                        if dtout == 0 and l3["xd"].get(tq - 1):
                            v.wait_ge(d2.h, l3["xd"][tq - 1])
                        v.tensor_tensor(xstage[g2][:, 128 * dtout:128 * (dtout + 1)],
                                        xstage[g2][:, 128 * dtout:128 * (dtout + 1)],
                                        khPS[:, 0:128], ALU.add).then_inc(vS.h, 1)
                    VE(l3_acc); vS.n += 1
                    l3["acc_v"] = vS.n
                l3["ev3_v"] = vS.n
            res_v = vS.n
            l3["res"][tq] = res_v

            def l3_out(g, tq=tq, res_v=res_v):
                g.wait_ge(vS.h, res_v)
                dma(g, d2, out_d[256 * tq:256 * tq + 128, :], xstage[0][:, :])
                dma(g, d2, out_d[256 * tq + 128:256 * tq + 256, :], xstage[1][:, :])
            GP(l3_out)
            l3["xd"][tq] = d2.n
        p5_done = dict(t=tS.n, v=vS.n, a=aS.n, d2=d2.n)

    if DEBUG:
        parcel_o = dout("parcel_o", [16, 128, 12288], BF16)
        q_o = dout("q_o", [128, H * T], BF16)
        etaG_o = dout("etaG_o", [128, 512])
        betaT_o = dout("betaT_o", [H, T])
        gamw_o = dout("gamw_o", [1, 8])
        sxg_o = dout("sxg_o", [128, 32])
        nxT_o = dout("nxT_o", [128, 16 * T], BF16)
        sg_o = dout("sg_o", [16, 128, D], BF16)
        xnT_o = dout("xnT_o", [128, 16 * TR], BF16)
        nxrm_o = dout("nxrm_o", [128, D], BF16)
        gxsx_o = dout("gxsx_o", [128, 32])
        smallA_o = dout("smallA_o", [128, 64])
        opre_o = dout("opre_o", [8, 128, D], BF16)
        h_o = dout("h_o", [8, 128, 8192])
        knG_o = dout("knG_o", [128, 128, 512], BF16)
        vstage_o = dout("vstage_o", [128, 2048], BF16)
        rmst_o = dout("rmst_o", [128, 1024], BF16)
        vG_o = dout("vG_o", [128, 128, 512], BF16)
        nhT_o = dout("nhT_o", [64, 128, 1024], BF16)
        sxh_o = dout("sxh_o", [1, 1024])

        def dbg_dump(sy):
            sy.wait_ge(vS.h, vS.n)
            sy.wait_ge(aS.h, aS.n)
            sy.wait_ge(tS.h, tS.n)
            sy.wait_ge(d2.h, d2.n)
            dma(sy, d1, etaG_o, etaG[:, :])
            dma(sy, d1, betaT_o, betaT[:, :])
            dma(sy, d1, gamw_o, rowA[0:1, 0:8])
            dma(sy, d1, sxg_o, sxg_col[:, :])
            dma(sy, d1, xnT_o, xnT[:, :])
            dma(sy, d1, nxrm_o, nxrm[:, :])
            dma(sy, d1, gxsx_o[:, 0:16], gx_col[:, :])
            dma(sy, d1, gxsx_o[:, 16:32], sx_col[:, :])
            dma(sy, d1, smallA_o, smallA[:, :])
        SY(dbg_dump)

        def dbg_dump2(g):
            g.wait_ge(vS.h, vS.n)
            g.wait_ge(d2.h, d2.n - 16)
            dma(g, d2, nxT_o, nxT_dram[:, :])
            dma(g, d2, knG_o, knG_dram[:, :, :])
            dma(g, d2, vstage_o, vstage[:, :])
            dma(g, d2, rmst_o, rmstage[1][:, :])
            dma(g, d2, vG_o, vG_dram[:, :, :])
            if "p5" in BUILD_PARTS:
                dma(g, d2, h_o, h_dram[:, :, :])
                dma(g, d2, nhT_o, nhT_dram[:, :, :])
                dma(g, d2, sxh_o, sxhrowF[0:1, :])
            dma(g, d2, parcel_o, parcel_dram[:, :, :])
            if "p3" in BUILD_PARTS:
                dma(g, d2, opre_o, opre_dram[:, :, :])
        GP(dbg_dump2)

    if DEBUG2 and "p5" in BUILD_PARTS:
        def dbg2_dump(g):
            g.wait_ge(vS.h, vS.n)
            g.wait_ge(d2.h, d2.n)
            dma(g, d2, dbg["h_o2"], h_dram[:, :, :])
            dma(g, d2, dbg["nhT_o2"], nhT_dram[:, :, :])
        GP(dbg2_dump)

    outx = nc.dram_tensor("outx", [1, 4], F32, kind="ExternalOutput").ap()

    def fin(sy):
        sy.wait_ge(d1.h, d1.n)
        sy.wait_ge(d2.h, d2.n)
        sy.wait_ge(d3.h, d3.n)
        dma(sy, d1, outx, rowA[0:1, 0:4])
        sy.wait_ge(d1.h, d1.n)
    SY(fin)

    st.close()
    return nc


def host_prep(inputs):
    ln_w = np.asarray(inputs["L0_attn_ln_w"], np.float32)
    ln_b = np.asarray(inputs["L0_attn_ln_b"], np.float32)
    lnwG = np.zeros((128, 512), np.float32)
    lnbG = np.zeros((128, 512), np.float32)
    for ph in range(4):
        for g in range(4):
            h = 4 * ph + g
            lnwG[32 * ph:32 * ph + 16, 128 * g:128 * (g + 1)] = ln_w[h][None, :]
            lnbG[32 * ph:32 * ph + 16, 128 * g:128 * (g + 1)] = ln_b[h][None, :]
    lnwT = np.ascontiguousarray(ln_w.T)
    ident = np.eye(128, dtype=np.float32)
    kt = np.arange(128)[:, None]; qt = np.arange(128)[None, :]
    maskT = (((kt // 16) < (qt // 16)) | (((kt // 16) == (qt // 16)) & (kt <= qt))).astype(np.float32)
    return dict(
        wq=np.ascontiguousarray(inputs["L0_attn_Wq"], np.float32),
        wk=np.ascontiguousarray(inputs["L0_attn_Wk"], np.float32),
        wv=np.ascontiguousarray(inputs["L0_attn_Wv"], np.float32),
        wo=np.ascontiguousarray(inputs["L0_attn_Wo"], np.float32),
        wbeta=np.ascontiguousarray(inputs["L0_attn_Wbeta"], np.float32),
        wg1=np.ascontiguousarray(inputs["L0_attn_Wg1"], np.float32),
        wg2=np.ascontiguousarray(inputs["L0_attn_Wg2"], np.float32),
        lnwG=lnwG, lnbG=lnbG, lnwT=lnwT, ident=ident, maskT=maskT,
        wrm=np.tile(ln_w.reshape(1, 2048), (128, 1)).astype(np.float32),
        brm=np.tile(ln_b.reshape(1, 2048), (128, 1)).astype(np.float32),
        nwrm=np.tile(np.asarray(inputs["L0_attn_norm_w"], np.float32).reshape(1, 2048), (128, 1)),
        gwT=np.ascontiguousarray(np.asarray(inputs["L0_ffn_gate_w"], np.float32).T),
        vwT=np.ascontiguousarray(np.asarray(inputs["L0_ffn_val_w"], np.float32).T),
        owT=np.ascontiguousarray(np.asarray(inputs["L0_ffn_out_w"], np.float32).T),
        gbbc=np.ascontiguousarray(np.tile(np.asarray(inputs["L0_ffn_gate_b"], np.float32)[None, :], (128, 1))),
        vbbc=np.ascontiguousarray(np.tile(np.asarray(inputs["L0_ffn_val_b"], np.float32)[None, :], (128, 1))),
        obm=np.ascontiguousarray(np.asarray(inputs["L0_ffn_out_b"], np.float32).reshape(16, 128).T),
    )


_prog_cache = {}


def get_program():
    if "nc" not in _prog_cache:
        _prog_cache["nc"] = build_program()
    return _prog_cache["nc"]


def make_meta(half):
    m = np.zeros((1, 64), np.int32)
    m[0, 0] = 1024 * half
    m[0, 1] = 8 * half
    for B in range(8):
        m[0, 8 + B] = 8 * half + B
        m[0, 16 + B] = 128 * (8 * half + B)
        m[0, 24 + B] = 8 * (8 * half + B)
    m[0, 3] = half
    for tq in range(4):
        m[0, 32 + tq] = 1024 * half + 256 * tq
        m[0, 36 + tq] = 1024 * half + 256 * tq + 128
        m[0, 40 + tq] = 4 * half + tq
    return m


def kernel(**inputs):
    x = np.asarray(inputs["L0_attn_x"], np.float32)
    base = host_prep(inputs)
    nc = get_program()
    in_maps = []
    for c in range(8):
        b, half = c // 2, c % 2
        m = dict(base)
        m["x"] = np.ascontiguousarray(x[b])
        m["meta"] = make_meta(half)
        in_maps.append(m)
    res = run_bass_kernel_spmd(nc, in_maps, core_ids=list(range(8)))
    out = np.empty((4, 2048, 2048), np.float32)
    for c in range(8):
        b, half = c // 2, c % 2
        out[b, 1024 * half:1024 * (half + 1), :] = np.asarray(res.results[c]["out"], np.float32)
    return out


# version 09
# speedup vs baseline: 1.0554x; 1.0554x over previous
import numpy as np
import concourse.bass as bass
import concourse.mybir as mybir
from concourse.bass_utils import run_bass_kernel_spmd
from contextlib import ExitStack

F32 = mybir.dt.float32
BF16 = mybir.dt.bfloat16
I32 = mybir.dt.int32
AF = mybir.ActivationFunctionType
ALU = mybir.AluOpType
AX = mybir.AxisListType.X

MAGIC = float(np.float32(1.5 * 2 ** 23))
D = 2048
T = 2048
H = 16
DH = 128
TR = 1024
EPS_LN = 1e-5
EPS_A = 1e-6
DEBUG = False
DEBUG2 = False
BUILD_PARTS = {"p0", "kvq", "gate", "beta", "eta", "gw", "scan", "p3", "p5"}


def build_program():
    nc = bass.Bass("TRN2", target_bir_lowering=False, debug=False)
    st = ExitStack()

    def din(name, shape, dt=F32):
        return nc.dram_tensor(name, shape, dt, kind="ExternalInput").ap()

    x_in = din("x", [T, D])
    meta_in = din("meta", [1, 64], I32)
    wq_d = din("wq", [D, D]); wk_d = din("wk", [D, D]); wv_d = din("wv", [D, D])
    wo_d = din("wo", [D, D])
    wbeta_d = din("wbeta", [D, H])
    wg1_d = din("wg1", [D, DH]); wg2_d = din("wg2", [DH, D])
    lnwG_d = din("lnwG", [128, 512]); lnbG_d = din("lnbG", [128, 512])
    lnwT_d = din("lnwT", [128, H])
    ident_d = din("ident", [128, 128])
    maskT_d = din("maskT", [128, 128])
    wrm_d = din("wrm", [128, D]); brm_d = din("brm", [128, D]); nwrm_d = din("nwrm", [128, D])
    gbbc_d = din("gbbc", [128, 4 * D]); vbbc_d = din("vbbc", [128, 4 * D])
    ob_d = din("obm", [128, 16])
    gwT_d = din("gwT", [D, 4 * D]); vwT_d = din("vwT", [D, 4 * D])
    owT_d = din("owT", [4 * D, D])

    out_d = nc.dram_tensor("out", [TR, D], F32, kind="ExternalOutput").ap()

    dbg = {}
    def dout(name, shape, dt=F32):
        dbg[name] = nc.dram_tensor(name, shape, dt, kind="ExternalOutput").ap()
        return dbg[name]

    betaT_dram = nc.dram_tensor("betaT_dram", [H, T + 16], F32).ap()
    nxT_dram = nc.dram_tensor("nxT_dram", [2, 128, 16 * TR], BF16).ap()
    h_dram = nc.dram_tensor("h_dram", [8, 128, 8192], F32).ap()
    nhT_dram = nc.dram_tensor("nhT_dram", [64, 128, 1024], BF16).ap()
    terno_dram = nc.dram_tensor("terno_dram", [16, 64, 128, 128], BF16).ap()
    parcel_dram = nc.dram_tensor("parcel_dram", [16, 128, 12288], BF16).ap()
    sxg_dram = nc.dram_tensor("sxg_dram", [2, 128, 16], F32).ap()
    opre_dram = nc.dram_tensor("opre_dram", [8, 128, D], BF16).ap()
    vtstage_dram = nc.dram_tensor("vtstage_dram", [128, 4096], BF16).ap()
    xmir_dram = nc.dram_tensor("xmir_dram", [8, 128, 2 * D], F32).ap()
    knG_dram = nc.dram_tensor("knG_dram", [128, 128, 512], BF16).ap()
    vG_dram = nc.dram_tensor("vG_dram", [128, 128, 512], BF16).ap()

    def sbuf(name, shape, dt):
        return st.enter_context(nc.sbuf_tensor(name, shape, dt))

    def psum(name, shape, dt=F32):
        return st.enter_context(nc.psum_tensor(name, shape, dt))


    etaG = sbuf("etaG_sb", [128, 512], F32)
    WG = sbuf("WG", [128, 512], F32); BG = sbuf("BG", [128, 512], F32)
    W2G = sbuf("W2G", [128, 512], F32); WBG = sbuf("WBG", [128, 512], F32)
    identb = sbuf("identb", [128, 128], BF16)
    identf = sbuf("identf", [128, 128], F32)
    maskTb = sbuf("maskTb", [128, 128], BF16)
    meta_sb = sbuf("meta_sb", [1, 64], I32)
    lnwT = sbuf("lnwT_sb", [128, H], F32)
    ob_sb = sbuf("ob_sb", [128, 16], F32)
    gx_col = sbuf("gx_col", [128, 16], F32)
    sx_col = sbuf("sx_col", [128, 16], F32)
    sxg_col = sbuf("sxg_col", [128, 32], F32)
    gwsum = sbuf("gwsum", [128, 384], F32)
    gamw128 = sbuf("gamw128", [128, 4], F32)
    smallA = sbuf("smallA", [128, 64], F32)
    onecol = sbuf("onecol", [128, 1], F32)
    cbuf = sbuf("cbuf", [128, 8], F32)
    rowA = sbuf("rowA", [1, 512], F32)
    onerow = sbuf("onerow", [1, 128], F32)
    rstdrow = sbuf("rstdrow", [1, 512], F32)
    shadow = sbuf("shadow", [128, H * 128], BF16)
    smallB = sbuf("smallB", [128, 64], F32)
    sxhrowF = sbuf("sxhrowF", [1, 1024], F32)

    ARF = sbuf("ARF", [128, 16384], F32)
    ARB = sbuf("ARB", [128, 45056], BF16)
    ibuf = sbuf("ibuf", [128, 2048], I32)[:, :]

    xrow = [ARF[:, 0:2048], ARF[:, 2048:4096]]
    xnf = ARF[:, 4096:6144]
    nxtmp = ARF[:, 6144:8192]
    sqbuf = ARF[:, 8192:10240]
    etasrc = ARF[:, 10240:10752]
    betaT = ARF[0:16, 10752:12800]
    wfbuf = [ARF[:, 12800:13824], ARF[:, 13824:14848]]
    xnT = ARB[:, 0:16384]
    xnb = ARB[:, 16384:18432]
    nxrm = ARB[:, 18432:20480]
    nxTsb = ARB[:, 20480:22528]
    wkh = [ARB[:, 22528:24576], ARB[:, 24576:26624]]
    t1T = ARB[:, 26624:28672]
    wbeta_sb = ARB[:, 28672:28928]
    wg2_sb = ARB[:, 28928:30976]
    stage = ARB[:, 30976:33024]
    rmstage = [ARB[:, 33024:34048], ARB[:, 34048:35072]]
    vstage = ARB[:, 35072:37120]
    ps_s = psum("ps_s", [128, 512])
    stA = ExitStack()

    def psumA(name, shape, dt=F32):
        return stA.enter_context(nc.psum_tensor(name, shape, dt))

    ps_t = [psumA("ps_t0", [128, 512], BF16), psumA("ps_t1", [128, 512], BF16)]
    ps_m = [psumA("ps_m0", [128, 512]), psumA("ps_m1", [128, 512])]
    ps_n = psumA("ps_n", [128, 512])

    class Sem:
        def __init__(self, name):
            self.h = st.enter_context(nc.semaphore(name))
            self.n = 0

    d1 = Sem("d1"); d2 = Sem("d2"); d3 = Sem("d3")
    tS = Sem("tS"); vS = Sem("vS"); aS = Sem("aS"); pS = Sem("pS")

    def fence(eng, sem, inst):
        inst.then_inc(sem.h, 1)
        sem.n += 1
        eng.wait_ge(sem.h, sem.n)

    blk = st.enter_context(nc.Block())

    def SY(fn):
        @blk.sync
        def _(e):
            fn(e)

    def GP(fn):
        @blk.gpsimd
        def _(e):
            fn(e)

    def TE(fn):
        @blk.tensor
        def _(e):
            fn(e)

    def VE(fn):
        @blk.vector
        def _(e):
            fn(e)

    def SC(fn):
        @blk.scalar
        def _(e):
            fn(e)

    def dma(q, sem, out, in_, **kw):
        q.dma_start(out=out, in_=in_, **kw).then_inc(sem.h, 16)
        sem.n += 16
        return sem.n

    xnT3 = xnT[:, :].rearrange("p (dt c) -> p dt c", dt=16)

    strip_sched = []
    for mi, md in enumerate((gwT_d, vwT_d, owT_d) if "gw" in BUILD_PARTS else ()):
        rows, cols = md.shape
        nch = (rows * cols) // (128 * 1024)
        cpr = cols // 1024
        for s_i in range(nch):
            strip_sched.append((mi, md, (s_i // cpr) * 128, (s_i % cpr) * 1024))
    gw_done_a = {}

    gwbufs = [wfbuf[0], wfbuf[1], ARF[:, 14848:15872]]

    def emit_gw_strip(idx):
        mi, md, r0, c0 = strip_sched[idx]
        buf = gwbufs[idx % 3]

        def gw_load(s, buf=buf, md=md, r0=r0, c0=c0, idx=idx):
            if idx - 3 in gw_done_a:
                s.wait_ge(vS.h, gw_done_a[idx - 3])
            dma(s, d3, buf[:, :], md[r0:r0 + 128, c0:c0 + 1024])
        SC(gw_load)
        w_rdy = d3.n

        def gw_acc(v, buf=buf, idx=idx, w_rdy=w_rdy):
            v.wait_ge(d3.h, w_rdy)
            v.wait_ge(vS.h, consts_v)
            v.reduce_sum(gwsum[:, idx:idx + 1], buf[:, :], axis=AX,
                         apply_absolute_value=True).then_inc(vS.h, 1)
        VE(gw_acc); vS.n += 1
        gw_done_a[idx] = vS.n


    gw_state = {"emitted": 0}


    def p0_consts(sy):
        dma(sy, d1, identf[:, :], ident_d[:, :])
        dma(sy, d1, WG[:, :], lnwG_d[:, :])
        dma(sy, d1, BG[:, :], lnbG_d[:, :])
        dma(sy, d1, lnwT[:, :], lnwT_d[:, :])
        dma(sy, d1, ob_sb[:, :], ob_d[:, :])
    SY(p0_consts)
    consts_done = d1.n

    def p0c_v(v):
        v.wait_ge(d1.h, consts_done)
        v.memset(onecol[:, :], 1.0)
        v.memset(onerow[:, :], 1.0)
        v.memset(cbuf[:, 0:1], EPS_LN)
        v.memset(cbuf[:, 1:2], EPS_A)
        v.memset(cbuf[:, 2:3], 0.0)
        v.memset(shadow[:, :], 0.0)
        v.memset(smallB[:, :], 0.0)
        v.memset(smallA[:, :], 0.0)
        v.memset(gwsum[:, :], 0.0)
        v.tensor_copy(identb[:, :], identf[:, :])
        v.tensor_tensor(W2G[:, :], WG[:, :], WG[:, :], ALU.mult)
        v.tensor_tensor(WBG[:, :], WG[:, :], BG[:, :],
                        ALU.mult).then_inc(vS.h, 1)
    VE(p0c_v); vS.n += 1
    consts_v = vS.n

    def p0_xmir(g):
        for t in range(16):
            c0 = 2048 * (t % 2)
            dma(g, d3, xmir_dram[t // 2, :, c0:c0 + 2048],
                x_in[128 * t:128 * (t + 1), :].rearrange("(o p) c -> o p c", o=1)[0])
    GP(p0_xmir)
    xmir_d = d3.n

    def p0_wload(g):
        dma(g, d2, meta_sb[:, :], meta_in[:, :])
        dma(g, d2, maskTb[:, :], maskT_d[:, :])
        dma(g, d2, wbeta_sb[:, :].rearrange("p (dt h) -> p dt h", dt=16),
            wbeta_d.rearrange("(dt p) h -> p dt h", p=128))
        dma(g, d2, wg2_sb[:, :], wg2_d[:, :])
    GP(p0_wload)
    wsmall_rdy = d2.n


    state = dict(a_xn={}, v_nx={}, t_xT={}, a_xT={}, t_nT={}, a_nT={},
                 bank_a={0: 0, 1: 0},
                 mm_bank_a={0: 0, 1: 0},
                 te_w={},
                 )

    def emit_half(hidx):
        tok0 = TR * hidx
        tile0 = 8 * hidx

        for ti in range(8):
            t = tile0 + ti
            xb = xrow[t % 2]

            def p0_dma(sy, t=t, xb=xb):
                if state["a_xn"].get(t - 2):
                    sy.wait_ge(aS.h, state["a_xn"][t - 2])
                if state["v_nx"].get(t - 2):
                    sy.wait_ge(vS.h, state["v_nx"][t - 2])
                dma(sy, d1, xb[:, :], x_in[128 * t:128 * (t + 1), :])
            SY(p0_dma)
            x_rdy = d1.n

            def p0_v1(v, xb=xb, x_rdy=x_rdy):
                v.wait_ge(d1.h, x_rdy)
                v.reduce_sum(smallA[:, 0:1], xb[:, :], axis=AX).then_inc(vS.h, 1)
            VE(p0_v1); vS.n += 1
            v_sum = vS.n

            def p0_a1(s, xb=xb, x_rdy=x_rdy, t=t):
                s.wait_ge(d1.h, x_rdy)
                s.wait_ge(vS.h, consts_v)
                s.wait_ge(tS.h, state.get("sqbuf_te", 0))
                s.activation(sqbuf[:, :], xb[:, :], AF.Square,
                             accum_out=smallA[:, 32 + t:33 + t]).then_inc(aS.h, 1)
            SC(p0_a1); aS.n += 1
            a_sq = aS.n

            def p0_v2(v, v_sum=v_sum, a_sq=a_sq, t=t):
                v.wait_ge(vS.h, v_sum)
                v.wait_ge(aS.h, a_sq)
                i1 = v.tensor_scalar(smallA[:, 4:5], smallA[:, 0:1], 1.0 / D, None, ALU.mult)
                i2 = v.tensor_scalar(smallA[:, 5:6], smallA[:, 32 + t:33 + t], 1.0 / D, None, ALU.mult)
                fence(v, vS, i2)
                i3 = v.tensor_tensor(smallA[:, 6:7], smallA[:, 4:5], smallA[:, 4:5], ALU.mult)
                fence(v, vS, i3)
                v.tensor_tensor(smallA[:, 7:8], smallA[:, 5:6], smallA[:, 6:7],
                                ALU.subtract).then_inc(vS.h, 1)
            VE(p0_v2); vS.n += 1
            v_var = vS.n

            def p0_a2(s, v_var=v_var):
                s.wait_ge(vS.h, v_var)
                s.activation(smallA[:, 8:9], smallA[:, 7:8], AF.Sqrt,
                             bias=cbuf[:, 0:1]).then_inc(aS.h, 1)
            SC(p0_a2); aS.n += 1
            a_sd = aS.n

            def p0_v3(v, a_sd=a_sd, xb=xb, t=t):
                v.wait_ge(aS.h, a_sd)
                if state["v_nx"].get(t - 1):
                    v.wait_ge(vS.h, state["v_nx"][t - 1])
                ir = v.reciprocal(smallA[:, 9:10], smallA[:, 8:9])
                fence(v, vS, ir)
                v.tensor_scalar(xnf[:, :], xb[:, :], smallA[:, 4:5], smallA[:, 9:10],
                                ALU.subtract, ALU.mult).then_inc(vS.h, 1)
            VE(p0_v3); vS.n += 1
            v_r = vS.n

            def p0_a3(s, xb=xb, v_r=v_r, t=t):
                s.wait_ge(vS.h, v_r)
                if state["t_xT"].get(t - 1):
                    s.wait_ge(tS.h, state["t_xT"][t - 1])
                s.copy(xnb[:, :], xnf[:, :]).then_inc(aS.h, 1)
            SC(p0_a3); aS.n += 1
            state["a_xn"][t] = aS.n
            a_xn = aS.n

            def p0_v4(v, t=t, a_xn=a_xn):
                v.wait_ge(aS.h, a_xn)
                v.reduce_max(gx_col[:, t:t + 1], xnf[:, :], axis=AX,
                             apply_absolute_value=True).then_inc(vS.h, 1)
            VE(p0_v4); vS.n += 1
            v_gx = vS.n

            def p0_v5(v, t=t, v_gx=v_gx):
                v.wait_ge(vS.h, v_gx)
                i1 = v.tensor_scalar(smallA[:, 12:13], gx_col[:, t:t + 1], 1e-5, None, ALU.max)
                fence(v, vS, i1)
                i2 = v.tensor_scalar(smallA[:, 13:14], smallA[:, 12:13], 1.0 / 127.0, None, ALU.mult)
                fence(v, vS, i2)
                v.reciprocal(sx_col[:, t:t + 1], smallA[:, 13:14]).then_inc(vS.h, 1)
            VE(p0_v5); vS.n += 1
            v_sx = vS.n

            def p0_v6(v, t=t, v_sx=v_sx):
                v.wait_ge(vS.h, v_sx)
                if state["t_nT"].get(t - 1):
                    v.wait_ge(tS.h, state["t_nT"][t - 1])
                v.tensor_scalar(nxtmp[:, :], xnf[:, :], sx_col[:, t:t + 1],
                                None, ALU.mult)
                v.tensor_copy(ibuf, nxtmp[:, :])
                v.tensor_copy(nxrm[:, :], ibuf).then_inc(vS.h, 1)
            VE(p0_v6); vS.n += 1
            state["v_nx"][t] = vS.n
            v_nx = vS.n

            for grp in range(4):
                bi = grp % 2
                bank = ps_t[bi]

                def p0_t(te, grp=grp, bank=bank, bi=bi, a_xn=a_xn):
                    te.wait_ge(aS.h, a_xn)
                    te.wait_ge(aS.h, state["bank_a"][bi])
                    for i in range(4):
                        dt = 4 * grp + i
                        inst = te.transpose(bank[:, 128 * i:128 * (i + 1)],
                                            xnb[:, 128 * dt:128 * (dt + 1)], identb[:, :])
                    inst.then_inc(tS.h, 1)
                TE(p0_t); tS.n += 1
                t_T = tS.n
                state["t_xT"][t] = tS.n

                def p0_ae(s, grp=grp, bank=bank, ti=ti, t_T=t_T):
                    s.wait_ge(tS.h, t_T)
                    outap = xnT3[:, 4 * grp:4 * grp + 4, 128 * ti:128 * (ti + 1)]
                    s.copy(outap, bank[:, :].rearrange("p (i c) -> p i c", i=4)).then_inc(aS.h, 1)
                SC(p0_ae); aS.n += 1
                state["bank_a"][bi] = aS.n
                state["a_xT"][t] = aS.n

            for grp in range(4):
                bi = grp % 2
                bank = ps_t[bi]

                def p0_tn(te, grp=grp, bank=bank, bi=bi, v_nx=v_nx):
                    te.wait_ge(vS.h, v_nx)
                    te.wait_ge(aS.h, state["bank_a"][bi])
                    for i in range(4):
                        dt = 4 * grp + i
                        inst = te.transpose(bank[:, 128 * i:128 * (i + 1)],
                                            nxrm[:, 128 * dt:128 * (dt + 1)], identb[:, :])
                    inst.then_inc(tS.h, 1)
                TE(p0_tn); tS.n += 1
                t_T = tS.n
                state["t_nT"][t] = tS.n

                def p0_ane(s, grp=grp, bank=bank, t_T=t_T):
                    s.wait_ge(tS.h, t_T)
                    outap = nxTsb[:, :].rearrange("p (i c) -> p i c", i=16)[:, 4 * grp:4 * grp + 4, :]
                    s.copy(outap, bank[:, :].rearrange("p (i c) -> p i c", i=4)).then_inc(aS.h, 1)
                SC(p0_ane); aS.n += 1
                state["bank_a"][bi] = aS.n
                state["a_nT"][t] = aS.n

            a_nT = state["a_nT"][t]

            def p0_gnx(g, t=t, ti=ti, a_nT=a_nT, tok0=tok0):
                g.wait_ge(aS.h, a_nT)
                outap = nxT_dram[t // 8, :, :].rearrange("p (dt c) -> p dt c", dt=16)[:, :, 128 * ti: 128 * (ti + 1)]
                dma(g, d2, outap, nxTsb[:, :].rearrange("p (i c) -> p i c", i=16))
            GP(p0_gnx)
            nx_dma = d2.n

            def p0_guard(s, nx_dma=nx_dma):
                s.wait_ge(d2.h, nx_dma)
            SC(p0_guard)

        xn_half_a = state["a_xT"][tile0 + 7]
        xn_half_t = state["t_xT"][tile0 + 7]

        for c in range(2 if "beta" in BUILD_PARTS else 0):
            def p1_beta_t(te, c=c):
                te.wait_ge(d2.h, wsmall_rdy)
                te.wait_ge(aS.h, xn_half_a)
                for dt in range(16):
                    inst = te.matmul(ps_s[0:16, :], wbeta_sb[:, 16 * dt:16 * (dt + 1)],
                                     xnT3[:, dt, 512 * c:512 * (c + 1)],
                                     start=(dt == 0), stop=(dt == 15))
                inst.then_inc(tS.h, 1)
            TE(p1_beta_t); tS.n += 1
            tv = tS.n

            def p1_beta_a(s, c=c, tv=tv, tok0=tok0):
                s.wait_ge(tS.h, tv)
                s.activation(betaT[:, tok0 + 512 * c: tok0 + 512 * (c + 1)],
                             ps_s[0:16, :], AF.Sigmoid).then_inc(aS.h, 1)
            SC(p1_beta_a); aS.n += 1
            av = aS.n

            def p1_beta_g(te, av=av):
                te.wait_ge(aS.h, av)
            TE(p1_beta_g)

        def head_proj(tag, w_dram, park, act_func, norm=False):
            for h in range(H):
                if "gw" in BUILD_PARTS:
                    for _ in range(4):
                        if gw_state["emitted"] < len(strip_sched):
                            emit_gw_strip(gw_state["emitted"])
                            gw_state["emitted"] += 1
                buf = wkh[h % 2]

                def wl(g, h=h, buf=buf, w_dram=w_dram, tag=tag):
                    key = (tag, h - 2)
                    if key in state["te_w"]:
                        g.wait_ge(tS.h, state["te_w"][key])
                    src = w_dram[:, 128 * h:128 * (h + 1)].rearrange("(dt p) c -> p dt c", p=128)
                    dma(g, d2, buf[:, :].rearrange("p (dt c) -> p dt c", dt=16), src)
                GP(wl)
                w_rdy = d2.n

                for c in range(2):
                    bi = c % 2
                    bank = ps_m[bi]

                    def mm(te, h=h, c=c, buf=buf, bank=bank, bi=bi, w_rdy=w_rdy):
                        te.wait_ge(d2.h, w_rdy)
                        te.wait_ge(aS.h, xn_half_a)
                        te.wait_ge(aS.h, state["mm_bank_a"][bi])
                        for dt in range(16):
                            inst = te.matmul(bank[:, :], buf[:, 128 * dt:128 * (dt + 1)],
                                             xnT3[:, dt, 512 * c:512 * (c + 1)],
                                             start=(dt == 0), stop=(dt == 15))
                        inst.then_inc(tS.h, 1)
                    TE(mm); tS.n += 1
                    state["te_w"][(tag, h)] = tS.n
                    mm_done = tS.n

                    def ev(s, c=c, bank=bank, mm_done=mm_done, park=park):
                        s.wait_ge(tS.h, mm_done)
                        s.wait_ge(tS.h, state.get("rm_t_last", 0))
                        s.wait_ge(d2.h, state.get("park_dma", 0))
                        dstb = vstage if park == "v" else stage
                        s.activation(dstb[:, 512 * c:512 * (c + 1)], bank[:, :],
                                     act_func).then_inc(aS.h, 1)
                    SC(ev); aS.n += 1
                    state["mm_bank_a"][bi] = aS.n

                if norm:
                    for c in range(2):
                        ev_a = aS.n

                        def kn_sq(s, c=c, ev_a=ev_a):
                            s.wait_ge(aS.h, ev_a)
                            s.wait_ge(tS.h, state.get("sqbuf_te", 0))
                            s.activation(sqbuf[:, 0:512], stage[:, 512 * c:512 * (c + 1)],
                                         AF.Square).then_inc(aS.h, 1)
                        SC(kn_sq); aS.n += 1
                        a_sq2 = aS.n

                        def kn_mm(te, a_sq2=a_sq2):
                            te.wait_ge(aS.h, a_sq2)
                            te.wait_ge(vS.h, state.get("kn_v", 0))
                            te.matmul(ps_n[0:1, 0:512], onecol[:, :], sqbuf[:, 0:512],
                                      start=True, stop=True).then_inc(tS.h, 1)
                        TE(kn_mm); tS.n += 1
                        t_mm = tS.n
                        state["sqbuf_te"] = t_mm

                        def kn_row_a(s, t_mm=t_mm):
                            s.wait_ge(tS.h, t_mm)
                            s.activation(rstdrow[0:1, 0:512], ps_n[0:1, 0:512], AF.Sqrt,
                                         bias=cbuf[0:1, 1:2]).then_inc(aS.h, 1)
                        SC(kn_row_a); aS.n += 1
                        a_row = aS.n

                        def kn_row_v(v, a_row=a_row):
                            v.wait_ge(aS.h, a_row)
                            v.reciprocal(rstdrow[0:1, 0:512],
                                         rstdrow[0:1, 0:512]).then_inc(vS.h, 1)
                        VE(kn_row_v); vS.n += 1
                        v_row = vS.n

                        def kn_bc(te, v_row=v_row):
                            te.wait_ge(vS.h, v_row)
                            te.wait_ge(vS.h, state.get("kn_v", 0))
                            te.matmul(ps_n[:, 0:512], onerow[0:1, :], rstdrow[0:1, 0:512],
                                      start=True, stop=True).then_inc(tS.h, 1)
                        TE(kn_bc); tS.n += 1
                        t_bc = tS.n

                        def kn_ap(v, c=c, t_bc=t_bc):
                            v.wait_ge(tS.h, t_bc)
                            v.tensor_tensor(stage[:, 512 * c:512 * (c + 1)],
                                            stage[:, 512 * c:512 * (c + 1)],
                                            ps_n[:, 0:512], ALU.mult).then_inc(vS.h, 1)
                        VE(kn_ap); vS.n += 1
                        state["kn_v"] = vS.n

                ev_done = aS.n
                kn_done = state.get("kn_v", 0)

                if park != "v":
                    def pk(g, h=h, ev_done=ev_done, kn_done=kn_done, park=park, tok0=tok0, norm=norm):
                        g.wait_ge(aS.h, ev_done)
                        if norm:
                            g.wait_ge(vS.h, kn_done)
                        B0 = 8 * (tok0 // TR)
                        coff = 0 if park == "k" else 2048
                        outap = parcel_dram.rearrange("B p f -> p B f")[:, B0:B0 + 8, coff + 128 * h:coff + 128 * (h + 1)]
                        dma(g, d2, outap, stage[:, 0:TR].rearrange("p (B c) -> p B c", B=8))
                    GP(pk)
                    state["park_dma"] = d2.n

                if park in ("k", "v"):
                    rmst = rmstage[h % 2]
                    rm_hist = state.setdefault("rm_hist", [])
                    if not norm:
                        def rm_fence(v, ev_done=ev_done):
                            v.wait_ge(aS.h, ev_done)
                            v.nop().then_inc(vS.h, 1)
                        VE(rm_fence); vS.n += 1
                        evf = vS.n
                    else:
                        evf = kn_done
                    for grp in range(2):
                        bi = grp % 2
                        bank = ps_t[bi]

                        def rm_t(te, h=h, grp=grp, bank=bank, bi=bi,
                                 ev_done=ev_done, evf=evf, park=park):
                            te.wait_ge(aS.h, ev_done)
                            te.wait_ge(vS.h, evf)
                            te.wait_ge(aS.h, state["bank_a"][bi])
                            srcb = vstage if park == "v" else stage
                            for i in range(4):
                                tb = 4 * grp + i
                                inst = te.transpose(bank[:, 128 * i:128 * (i + 1)],
                                                    srcb[:, 128 * tb:128 * (tb + 1)],
                                                    identb[:, :])
                            inst.then_inc(tS.h, 1)
                        TE(rm_t); tS.n += 1
                        t_T = tS.n
                        state["rm_t_last"] = tS.n

                        def rm_e(s, grp=grp, bank=bank, rmst=rmst, t_T=t_T):
                            s.wait_ge(tS.h, t_T)
                            if len(rm_hist) >= 2:
                                s.wait_ge(d2.h, rm_hist[-2][1])
                            outap = rmst.rearrange("p (tb e) -> p tb e", tb=8)[:, 4 * grp:4 * grp + 4, :]
                            s.copy(outap, bank[:, :].rearrange("p (i c) -> p i c", i=4)).then_inc(aS.h, 1)
                        SC(rm_e); aS.n += 1
                        state["bank_a"][bi] = aS.n
                    rme_done = aS.n

                    def rm_d(g, h=h, rmst=rmst, rme_done=rme_done, park=park, hidx=hidx):
                        g.wait_ge(aS.h, rme_done)
                        rmd = knG_dram if park == "k" else vG_dram
                        pp, gg = h // 4, h % 4
                        for tb in range(8):
                            j0 = 8 * (8 * hidx + tb)
                            outap = bass.AP(rmd.tensor,
                                            j0 * 65536 + pp * 16384 + 128 * gg,
                                            [[65536, 8], [512, 16], [1, 128]])
                            dma(g, d2, outap, rmst[:, 128 * tb:128 * (tb + 1)])
                    GP(rm_d)
                    rm_hist.append(("d2", d2.n))
                    state["rm_d1"] = d2.n

        if "kvq" in BUILD_PARTS:
            head_proj("k", wk_d, "k", AF.Identity, norm=True)
            head_proj("v", wv_d, "v", AF.Identity)
            head_proj("q", wq_d, "q", AF.Silu)
        state[("kvq_done", hidx)] = d2.n

        if "gate" not in BUILD_PARTS:
            state[("half_done", hidx)] = dict(t=tS.n, a=aS.n, v=vS.n, d2=d2.n)
            return
        def p1_wg1(g):
            key = ("q", H - 2)
            if ("q", H - 2) in state["te_w"]:
                g.wait_ge(tS.h, state["te_w"][key])
            src = wg1_d.rearrange("(dt p) c -> p dt c", p=128)
            dma(g, d2, wkh[0][:, :].rearrange("p (dt c) -> p dt c", dt=16), src)
        GP(p1_wg1)
        wg1_rdy = d2.n

        for c in range(2):
            bi = c % 2
            bank = ps_m[bi]

            def p1_t1(te, c=c, bank=bank, bi=bi, wg1_rdy=wg1_rdy):
                te.wait_ge(d2.h, wg1_rdy)
                te.wait_ge(aS.h, state["mm_bank_a"][bi])
                for dt in range(16):
                    inst = te.matmul(bank[:, :], wkh[0][:, 128 * dt:128 * (dt + 1)],
                                     xnT3[:, dt, 512 * c:512 * (c + 1)],
                                     start=(dt == 0), stop=(dt == 15))
                inst.then_inc(tS.h, 1)
            TE(p1_t1); tS.n += 1
            tv = tS.n

            def p1_t1e(s, c=c, bank=bank, tv=tv, tok0=tok0):
                s.wait_ge(tS.h, tv)
                s.copy(t1T[:, tok0 + 512 * c: tok0 + 512 * (c + 1)],
                       bank[:, :]).then_inc(aS.h, 1)
            SC(p1_t1e); aS.n += 1
            state["mm_bank_a"][bi] = aS.n
        t1_done = aS.n

        for ti in range(8):
            t = tile0 + ti

            for c in range(4):
                bi = c % 2
                bank = ps_m[bi]

                def p1_sg(te, t1_done=t1_done, tok0=tok0, ti=ti, c=c, bank=bank, bi=bi):
                    te.wait_ge(aS.h, t1_done)
                    te.wait_ge(d2.h, wsmall_rdy)
                    te.wait_ge(d2.h, state.get("sg_dma", 0))
                    te.wait_ge(aS.h, state["mm_bank_a"][bi])
                    te.matmul(bank[:, :],
                              t1T[:, tok0 + 128 * ti: tok0 + 128 * (ti + 1)],
                              wg2_sb[:, 512 * c:512 * (c + 1)],
                              start=True, stop=True).then_inc(tS.h, 1)
                TE(p1_sg); tS.n += 1
                sg_t = tS.n

                def p1_sga(s, sg_t=sg_t, c=c, bank=bank):
                    s.wait_ge(tS.h, sg_t)
                    s.wait_ge(tS.h, state.get("rm_t_last", 0))
                    s.activation(stage[:, 512 * c:512 * (c + 1)],
                                 bank[:, :], AF.Sigmoid).then_inc(aS.h, 1)
                SC(p1_sga); aS.n += 1
                state["mm_bank_a"][bi] = aS.n
            sg_a = aS.n

            def p1_sgd(g, t=t, sg_a=sg_a):
                g.wait_ge(aS.h, sg_a)
                dma(g, d2, parcel_dram[t, :, 6144:8192], stage[:, :])
            GP(p1_sgd)
            state["sg_dma"] = d2.n
        state[("half_done", hidx)] = dict(t=tS.n, a=aS.n, v=vS.n, d2=d2.n)

    if "p0" in BUILD_PARTS:
        emit_half(0)
        emit_half(1)

    EMIT_ETA = ("eta" in BUILD_PARTS) and ("beta" in BUILD_PARTS)
    beta_all_a = aS.n
    def p1_beta_dma(g):
        g.wait_ge(aS.h, aS.n)
        dma(g, d2, betaT_dram[:, 0:T], betaT[:, :])
    if EMIT_ETA:
        GP(p1_beta_dma)
    betaT_d_done = d2.n

    def p1_eta_dma(g):
        g.wait_ge(d2.h, betaT_d_done)
        for gidx in range(4):
            src = bass.AP(betaT_dram.tensor, gidx * (T + 16),
                          [[16, 128], [4 * (T + 16), 4], [1, 32]])
            dma(g, d2, etasrc[:, 128 * gidx:128 * (gidx + 1)].rearrange("p (a c) -> p a c", a=4), src)
    if EMIT_ETA:
        GP(p1_eta_dma)
    eta_src_done = d2.n

    def p1_eta_t(te):
        te.wait_ge(d2.h, eta_src_done)
        te.wait_ge(d1.h, consts_done)
        for gidx in range(4):
            inst = te.matmul(ps_m[0][:, 128 * gidx:128 * (gidx + 1)],
                             etasrc[:, 128 * gidx:128 * (gidx + 1)], identf[:, :],
                             start=True, stop=True)
        inst.then_inc(tS.h, 1)
    if EMIT_ETA:
        TE(p1_eta_t); tS.n += 1
    eta_t = tS.n

    def p1_eta_v(v):
        v.wait_ge(tS.h, eta_t)
        v.tensor_copy(etaG[:, :], ps_m[0][:, :])
        v.tensor_scalar(etaG[:, :], etaG[:, :], -2.0, None,
                        ALU.mult).then_inc(vS.h, 1)
    if EMIT_ETA:
        VE(p1_eta_v); vS.n += 1
    eta_done = vS.n

    stA.close()
    khsq = ARF[:, 0:512]
    khtmp = ARF[:, 512:1024]
    khh = ARF[:, 1024:1536]
    u2b = ARF[:, 1536:2048]
    u3b = ARF[:, 2048:2560]
    gwbuf = ARF[:, 2560:3584]
    x1b = ARF[:, 3584:4096]
    x2b = ARF[:, 4096:4608]
    x3b = ARF[:, 4608:5120]
    vtF = ARF[:, 5120:5632]
    kblk = ARB[:, 0:2048]
    krmG = [ARB[:, 2048:2560], ARB[:, 2560:3072]]
    vGb = [ARB[:, 3072:3584], ARB[:, 3584:4096]]
    tgtGb = ARB[:, 4096:4608]
    vtG = [ARB[:, 6656:7168], ARB[:, 7168:7680]]
    vtBD = ARB[:, 7680:9728]
    SCAN = "scan" in BUILD_PARTS
    if SCAN:
        S_ps = [psum(f"S_ps{g}", [128, 512]) for g in range(4)]
        khPS = psum("khPS", [128, 512])
        tgtPS = psum("tgtPS", [128, 512], BF16)
        krmPS = psum("krmPS", [128, 512], BF16)

        a_gate = aS.n
        v_gate = vS.n
        d2_gate = d2.n
        t_gate = tS.n

        def bc3(apx, n=128):
            return bass.AP(apx.tensor, apx.offset, list(apx.ap) + [[0, n]])

        khPS3 = khPS[:, :].rearrange("p (g e) -> p g e", g=4)
        gw3 = gwbuf[:, 0:512].rearrange("p (g e) -> p g e", g=4)
        khh3 = khh[:, :].rearrange("p (g e) -> p g e", g=4)
        x1b3 = x1b[:, :].rearrange("p (g e) -> p g e", g=4)
        x3b3 = x3b[:, :].rearrange("p (g e) -> p g e", g=4)
        x2b3 = x2b[:, :].rearrange("p (g e) -> p g e", g=4)
        eta3 = etaG[:, :].rearrange("p (g j) -> p g j", g=4)

        def s_clear(te):
            te.wait_ge(vS.h, consts_v)
            te.wait_ge(aS.h, a_gate)
            te.wait_ge(vS.h, v_gate)
            for g in range(4):
                inst = te.matmul(S_ps[g][0:1, 0:1], cbuf[0:1, 2:3], cbuf[0:1, 2:3],
                                 start=True, stop=True)
            inst.then_inc(tS.h, 1)
        TE(s_clear); tS.n += 1
        s_clear_t = tS.n

        def krm_zero_v_(v):
            v.wait_ge(tS.h, s_clear_t)
            v.memset(vtBD[:, :], 0.0)
            v.memset(krmG[0][:, :], 0.0)
            v.memset(krmG[1][:, :], 0.0)
            v.memset(vtG[0][:, :], 0.0)
            v.memset(vtG[1][:, :], 0.0).then_inc(vS.h, 1)
        VE(krm_zero_v_); vS.n += 1
        zb_v = vS.n
        krm_zero_t = s_clear_t

        if DEBUG:
            dout("vGearly_o", [128, 128, 512], BF16)
            def vg_snap(g):
                g.wait_ge(d2.h, state["rm_d1"])
                dma(g, d2, dbg["vGearly_o"], vG_dram[:, :, :])
            GP(vg_snap)

        def gpad_zero(g):
            g.wait_ge(vS.h, zb_v)
            for rmd in (knG_dram, vG_dram):
                for jo in range(16):
                    outap = bass.AP(rmd.tensor, jo * 8 * 65536 + 16 * 512,
                                    [[65536, 8], [16384, 4], [1, 8192]])
                    dma(g, d2, outap, vtBD[:, :])
        GP(gpad_zero)
        gpad_d = d2.n

        st_state = dict(a_sh=0, mm2=0, vt_dma={}, kblk_te=0, tgt_v=0, ck_dma=0,
                        mm2_t={}, vvt={}, gload={})

        def emit_gload(j):
            def gload(g, j=j):
                g.wait_ge(d2.h, state["rm_d1"])
                if j < 2:
                    g.wait_ge(d2.h, state[("kvq_done", 1)])
                    g.wait_ge(d2.h, gpad_d)
                else:
                    g.wait_ge(tS.h, st_state["mm2_t"][j - 2])
                    g.wait_ge(vS.h, st_state["vvt"][j - 2])
                dma(g, d2, krmG[j % 2][:, :], knG_dram[j, :, :])
                dma(g, d2, vGb[j % 2][:, :], vG_dram[j, :, :])
            GP(gload)
            st_state["gload"][j] = d2.n

        for B in range(16):
            def blk_dma(g, B=B):
                g.wait_ge(d2.h, state[("kvq_done", 1)])
                if st_state["kblk_te"]:
                    g.wait_ge(tS.h, st_state["kblk_te"])
                inap = parcel_dram.rearrange("B p f -> p B f")[:, B, 0:2048]
                dma(g, d2, kblk[:, :], inap)
            GP(blk_dma)
            blk_rdy = d2.n
            tgt_v = 0

            def ck_dump(g, B=B):
                if st_state["a_sh"]:
                    g.wait_ge(vS.h, st_state["a_sh"])
                else:
                    g.wait_ge(vS.h, consts_v)
                dma(g, d2, parcel_dram[B, :, 4096:6144], shadow[:, :])
            GP(ck_dump)
            st_state["ck_dma"] = d2.n

            for jj in range(8):
                j = 8 * B + jj
                vt_buf = vtG[j % 2]
                if j == 0:
                    emit_gload(0)
                    emit_gload(1)

                def te_a(te, jj=jj, blk_rdy=blk_rdy, j=j):
                    te.wait_ge(d2.h, blk_rdy)
                    if st_state["a_sh"]:
                        te.wait_ge(vS.h, st_state["a_sh"])
                    if j == 0:
                        te.wait_ge(tS.h, t_gate + 1)
                        te.wait_ge(d2.h, st_state["ck_dma"])
                    for g in range(4):
                        for p in range(4):
                            h = 4 * p + g
                            inst = te.matmul(khPS[32 * p:32 * p + 16, 128 * g:128 * (g + 1)],
                                             kblk[:, 128 * h + 16 * jj:128 * h + 16 * jj + 16],
                                             shadow[:, 128 * h:128 * (h + 1)],
                                             start=True, stop=True, tile_position=(0, 32 * p))
                    inst.then_inc(tS.h, 1)
                TE(te_a); tS.n += 1
                t_mm1 = tS.n
                st_state["kblk_te"] = t_mm1

                def po_tgt(g, j=j, t_mm1=t_mm1):
                    g.wait_ge(d2.h, st_state["gload"][j])
                    if st_state.get("u2_v"):
                        g.wait_ge(vS.h, st_state["u2_v"])
                    g.tensor_tensor(tgtGb[:, :], vGb[j % 2][:, :], krmG[j % 2][:, :],
                                    ALU.subtract).then_inc(pS.h, 1)
                GP(po_tgt); pS.n += 1
                p_tgt = pS.n

                def ve_a(v, t_mm1=t_mm1, j=j):
                    v.wait_ge(tS.h, t_mm1)
                    if j >= 2 and st_state["vt_dma"].get(j - 2):
                        v.wait_ge(d2.h, st_state["vt_dma"][j - 2])
                    v.tensor_copy(khtmp[:, :], khPS[:, :])
                    v.tensor_tensor(khsq[:, :], khtmp[:, :], khtmp[:, :], ALU.mult)
                    i1 = v.reduce_sum(smallB[:, 0:4],
                                      khtmp[:, :].rearrange("p (g e) -> p g e", g=4), axis=AX)
                    fence(v, vS, i1)
                    i2 = v.reduce_sum(smallB[:, 4:8], khsq[:, :].rearrange("p (g e) -> p g e", g=4), axis=AX)
                    fence(v, vS, i2)
                    v.tensor_scalar(smallB[:, 8:12], smallB[:, 0:4], 1.0 / 128.0, None, ALU.mult)
                    i3 = v.tensor_scalar(smallB[:, 12:16], smallB[:, 4:8], 1.0 / 128.0, None, ALU.mult)
                    fence(v, vS, i3)
                    i4 = v.tensor_tensor(smallB[:, 56:60], smallB[:, 8:12], smallB[:, 8:12], ALU.mult)
                    fence(v, vS, i4)
                    v.tensor_tensor(smallB[:, 16:20], smallB[:, 12:16], smallB[:, 56:60],
                                    ALU.subtract).then_inc(vS.h, 1)
                VE(ve_a); vS.n += 1
                v_var = vS.n

                def act_sd(s, v_var=v_var):
                    s.wait_ge(vS.h, v_var)
                    s.activation(smallB[:, 20:24], smallB[:, 16:20], AF.Sqrt,
                                 bias=cbuf[:, 1:2]).then_inc(aS.h, 1)
                SC(act_sd); aS.n += 1
                a_sd = aS.n

                def ve_b(v, a_sd=a_sd, j=j, p_tgt=p_tgt):
                    v.wait_ge(aS.h, a_sd)
                    ir = v.reciprocal(smallB[:, 24:28], smallB[:, 20:24])
                    fence(v, vS, ir)
                    im = v.tensor_tensor(smallB[:, 28:32], smallB[:, 8:12], smallB[:, 24:28], ALU.mult)
                    fence(v, vS, im)
                    v.tensor_tensor(khsq[:, :].rearrange("p (g e) -> p g e", g=4),
                                    khtmp[:, :].rearrange("p (g e) -> p g e", g=4),
                                    bc3(smallB[:, 24:28]), ALU.mult)
                    i5 = v.tensor_tensor(khh3, khsq[:, :].rearrange("p (g e) -> p g e", g=4),
                                         bc3(smallB[:, 28:32]), ALU.subtract)
                    i5.then_inc(vS.h, 1)
                    v.wait_ge(pS.h, p_tgt)
                    v.tensor_tensor(u2b[:, :], WG[:, :], tgtGb[:, :], ALU.mult)
                    v.tensor_tensor(u2b[:, :], WBG[:, :], u2b[:, :],
                                    ALU.subtract).then_inc(vS.h, 1)
                VE(ve_b); vS.n += 2
                v_khh = vS.n - 1
                v_u2 = vS.n
                st_state["u2_v"] = v_u2

                def po_a(g, v_khh=v_khh):
                    g.wait_ge(vS.h, v_khh)
                    g.tensor_tensor(u3b[:, :], W2G[:, :], khh[:, :], ALU.mult).then_inc(pS.h, 1)
                GP(po_a); pS.n += 1
                p_u3 = pS.n

                def ve_c(v, p_u3=p_u3, v_u2=v_u2, j=j):
                    v.wait_ge(pS.h, p_u3)
                    v.tensor_tensor(gwbuf[:, 0:512], u3b[:, :], u2b[:, :], ALU.add)
                    v.tensor_tensor(gwbuf[:, 512:1024], gwbuf[:, 0:512], khh[:, :], ALU.mult)
                    i6 = v.reduce_sum(smallB[:, 48:56],
                                      gwbuf[:, :].rearrange("p (g e) -> p g e", g=8), axis=AX)
                    fence(v, vS, i6)
                    i7 = v.tensor_scalar(smallB[:, 40:48], smallB[:, 48:56],
                                         1.0 / 128.0, None, ALU.mult)
                    fence(v, vS, i7)
                    ia = v.tensor_tensor(smallB[:, 32:36].rearrange("p (a b) -> p a b", b=1),
                                         smallB[:, 24:28].rearrange("p (a b) -> p a b", b=1),
                                         eta3[:, :, j:j + 1], ALU.mult)
                    fence(v, vS, ia)
                    v.tensor_tensor(x1b3, gw3, bc3(smallB[:, 40:44]),
                                    ALU.subtract).then_inc(vS.h, 1)
                VE(ve_c); vS.n += 1
                v_x3 = vS.n

                def po_b(g, v_x3=v_x3):
                    g.wait_ge(vS.h, v_x3)
                    g.tensor_tensor(x2b3, khh3, bc3(smallB[:, 44:48]), ALU.mult).then_inc(pS.h, 1)
                GP(po_b); pS.n += 1
                p_x2 = pS.n

                def ve_d(v, p_x2=p_x2, vt_buf=vt_buf):
                    v.wait_ge(pS.h, p_x2)
                    v.tensor_tensor(x3b[:, :], x1b[:, :], x2b[:, :], ALU.subtract)
                    iv = v.tensor_tensor(vtF[:, :].rearrange("p (g e) -> p g e", g=4),
                                         x3b[:, :].rearrange("p (g e) -> p g e", g=4),
                                         bc3(smallB[:, 32:36]), ALU.mult)
                    fence(v, vS, iv)
                    v.tensor_scalar(vt_buf[:, :], vtF[:, :], 1.0, None, ALU.mult)
                    vtBD4 = vtBD[:, :].rearrange("q (g pp e) -> q g pp e", g=4, pp=4)
                    for p in range(4):
                        outap = vtBD4[32 * p:32 * p + 16, :, p, :]
                        inap = vtF[32 * p:32 * p + 16, :].rearrange("q (g e) -> q g e", g=4)
                        inst = v.tensor_scalar(outap, inap, 1.0, None, ALU.mult)
                    inst.then_inc(vS.h, 1)
                VE(ve_d); vS.n += 1
                v_vt = vS.n

                def te_b(te, v_vt=v_vt, j=j):
                    te.wait_ge(vS.h, v_vt)
                    for g in range(4):
                        inst = te.matmul(S_ps[g][:, :],
                                         krmG[j % 2][:, 128 * g:128 * (g + 1)],
                                         vtBD[:, 512 * g:512 * (g + 1)],
                                         start=False, stop=False)
                    inst.then_inc(tS.h, 1)
                TE(te_b); tS.n += 1
                t_mm2 = tS.n
                st_state["mm2"] = t_mm2
                st_state["mm2_t"][j] = t_mm2
                st_state["vvt"][j] = v_vt

                def ve_sh(v, t_mm2=t_mm2, j=j):
                    v.wait_ge(tS.h, t_mm2)
                    if j % 8 == 0:
                        v.wait_ge(d2.h, st_state["ck_dma"])
                    for g in range(4):
                        inst = v.tensor_copy(shadow[:, :].rearrange("p (ph g2 e) -> p g2 ph e", ph=4, g2=4)[:, g, :, :],
                                             S_ps[g][:, :].rearrange("p (ph e) -> p ph e", ph=4))
                    inst.then_inc(vS.h, 1)
                VE(ve_sh); vS.n += 1
                st_state["a_sh"] = vS.n

                def gp_vt(g, v_vt=v_vt, j=j, vt_buf=vt_buf):
                    g.wait_ge(vS.h, v_vt)
                    c0 = 8192 + 512 * (j % 8)
                    dma(g, d2, parcel_dram[j // 8, :, c0:c0 + 512], vt_buf[:, :])
                GP(gp_vt)
                st_state["vt_dma"][j] = d2.n
                if j + 2 <= 127:
                    emit_gload(j + 2)


        scan_done = dict(t=tS.n, v=vS.n, a=aS.n, d2=d2.n)

    if "gw" in BUILD_PARTS:
        while gw_state["emitted"] < len(strip_sched):
            emit_gw_strip(gw_state["emitted"])
            gw_state["emitted"] += 1
    gw_all_v = vS.n
    n_strips = len(strip_sched)

    EMIT_GW = "gw" in BUILD_PARTS
    def gw_red(v):
        v.wait_ge(vS.h, gw_all_v)
        v.reduce_sum(smallA[:, 16:19],
                     gwsum[:, 0:n_strips].rearrange("p (m s) -> p m s", m=3),
                     axis=AX).then_inc(vS.h, 1)
    if EMIT_GW:
        VE(gw_red); vS.n += 1
    gwred_v = vS.n

    def gw_mm(te):
        te.wait_ge(vS.h, gwred_v)
        te.wait_ge(vS.h, consts_v)
        te.matmul(ps_s[0:1, 0:3], onecol[:, :], smallA[:, 16:19],
                  start=True, stop=True).then_inc(tS.h, 1)
    if EMIT_GW:
        TE(gw_mm); tS.n += 1
    gwmm_t = tS.n

    def gw_fin_v(v):
        v.wait_ge(tS.h, gwmm_t)
        ic = v.tensor_copy(rowA[0:1, 0:3], ps_s[0:1, 0:3])
        fence(v, vS, ic)
        v.tensor_scalar(rowA[0:1, 4:7], rowA[0:1, 0:3], 1.0 / (D * 4 * D), 1e-5,
                        ALU.mult, ALU.max).then_inc(vS.h, 1)
    if EMIT_GW:
        VE(gw_fin_v); vS.n += 1
    gwfin_v = vS.n

    def gw_bc(te):
        te.wait_ge(vS.h, gwfin_v)
        te.matmul(ps_s[:, 0:3], onerow[0:1, :], rowA[0:1, 4:7],
                  start=True, stop=True).then_inc(tS.h, 1)
    if EMIT_GW:
        TE(gw_bc); tS.n += 1
    gwbc_t = tS.n

    def gw_sxg(v):
        v.wait_ge(tS.h, gwbc_t)
        ic = v.tensor_copy(gamw128[:, 0:3], ps_s[:, 0:3])
        fence(v, vS, ic)
        ir = v.reciprocal(smallB[:, 32:48], sx_col[:, 0:16])
        fence(v, vS, ir)
        v.tensor_scalar(sxg_col[:, 0:16], smallB[:, 32:48], gamw128[:, 0:1], None, ALU.mult)
        v.tensor_scalar(sxg_col[:, 16:32], smallB[:, 32:48], gamw128[:, 1:2],
                        None, ALU.mult).then_inc(vS.h, 1)
    if EMIT_GW:
        VE(gw_sxg); vS.n += 1
    sxg_v = vS.n

    def gw_inv(v):
        v.wait_ge(vS.h, sxg_v)
        v.reciprocal(gamw128[:, 3:4], gamw128[:, 0:1])
        v.reciprocal(smallA[:, 20:21], gamw128[:, 1:2])
        ir = v.reciprocal(smallA[:, 22:23], gamw128[:, 2:3])
        fence(v, vS, ir)
        v.nop().then_inc(vS.h, 1)
    if EMIT_GW:
        VE(gw_inv); vS.n += 1
    gwinv_v = vS.n

    def gw_park(g):
        g.wait_ge(vS.h, sxg_v)
        for hf in range(2):
            inap = bass.AP(sxg_col[:, :].tensor, 8 * hf,
                           [sxg_col[:, :].ap[0], [16, 2], [1, 8]])
            dma(g, d2, sxg_dram[hf, :, :], inap)
    if EMIT_GW:
        GP(gw_park)


    if "p3" in BUILD_PARTS:
        obuf = ARF[:, 0:2048]
        sqm = ARF[:, 2048:4096]
        stc = ARF[:, 4096:4160]
        w_rm = ARB[:, 0:2048]
        b_rm = ARB[:, 2048:4096]
        nw_rm = ARB[:, 4096:6144]
        vtrm = ARB[:, 6144:8192]
        parcelb = ARB[:, 8192:20480]
        kblkP = parcelb[:, 0:2048]
        qblk = parcelb[:, 2048:4096]
        ckptb = parcelb[:, 4096:6144]
        sgateb = parcelb[:, 6144:8192]
        scSB = ARB[:, 20480:22528]
        qrm = ARB[:, 22528:24576]
        opre = ARB[:, 24576:26624]

        sd = scan_done
        p3s = dict(ld=0, sc_evac={}, qh_t={}, opark=0, qrm_v=0)

        def p3_consts(g):
            g.wait_ge(tS.h, sd["t"])
            g.wait_ge(aS.h, sd["a"])
            g.wait_ge(vS.h, sd["v"])
            dma(g, d2, w_rm[:, :], wrm_d[:, :])
            dma(g, d2, b_rm[:, :], brm_d[:, :])
            dma(g, d2, nw_rm[:, :], nwrm_d[:, :])
        GP(p3_consts)
        p3c_d = d2.n

        vt5 = vtstage_dram.rearrange("(pp cr) (jj g e) -> jj cr pp g e",
                                     pp=4, jj=8, g=4)

        for B in range(8):
            def p3_load(g, B=B):
                g.wait_ge(d2.h, p3c_d)
                if p3s["opark"]:
                    g.wait_ge(d2.h, p3s["opark"])
                with g.register(f"rb{B}") as rb:
                    g.reg_load(rb, meta_sb[0:1, 8 + B:9 + B])
                    Bg = g.snap(rb)
                    dma(g, d2, parcelb[:, :], parcel_dram[bass.ds(Bg, 1), :, :])
                g.wait_ge(d2.h, d2.n)
                dma(g, d2, vtstage_dram[:, :], parcelb[:, 8192:12288])
                g.wait_ge(d2.h, d2.n)
                for pp in range(4):
                    dma(g, d2, vtrm[:, 512 * pp:512 * (pp + 1)],
                        vt5[:, 0:16, pp, :, :])
            GP(p3_load)
            ld = d2.n

            for h in range(H):
                bank = S_ps[h // 4]
                col = 128 * (h % 4)

                def p3_sc(te, h=h, ld=ld):
                    te.wait_ge(d2.h, ld)
                    if p3s["sc_evac"].get(h - 1):
                        te.wait_ge(vS.h, p3s["sc_evac"][h - 1])
                    te.matmul(khPS[:, 0:128], kblkP[:, 128 * h:128 * (h + 1)],
                              qblk[:, 128 * h:128 * (h + 1)],
                              start=True, stop=True).then_inc(tS.h, 1)
                TE(p3_sc); tS.n += 1
                sc_t = tS.n

                def p3_scev(v, h=h, sc_t=sc_t):
                    v.wait_ge(tS.h, sc_t)
                    v.tensor_tensor(scSB[:, 128 * h:128 * (h + 1)], khPS[:, 0:128],
                                    maskTb[:, :], ALU.mult).then_inc(vS.h, 1)
                VE(p3_scev); vS.n += 1
                p3s["sc_evac"][h] = vS.n
                scv = vS.n

                def p3_qh(te, h=h, bank=bank, col=col, scv=scv, ld=ld, B=B):
                    te.wait_ge(vS.h, scv)
                    pp, g = h // 4, h % 4
                    te.matmul(bank[:, col:col + 128], qblk[:, 128 * h:128 * (h + 1)],
                              ckptb[:, 128 * h:128 * (h + 1)],
                              start=(h % 4 == 0), stop=False)
                    te.matmul(bank[:, col:col + 128], scSB[:, 128 * h:128 * (h + 1)],
                              vtrm[:, 512 * pp + 128 * g:512 * pp + 128 * g + 128],
                              start=False, stop=(h % 4 == 3)).then_inc(tS.h, 1)
                TE(p3_qh); tS.n += 1
                p3s["qh_t"][h] = tS.n
            qh_all = tS.n

            for bnk in range(4):
                def p3_qt(te, bnk=bnk, ld=ld):
                    te.wait_ge(d2.h, ld)
                    te.wait_ge(vS.h, p3s["qrm_v"])
                    for i in range(4):
                        h = 4 * bnk + i
                        inst = te.transpose(tgtPS[:, 128 * i:128 * (i + 1)],
                                            qblk[:, 128 * h:128 * (h + 1)], identb[:, :])
                    inst.then_inc(tS.h, 1)
                TE(p3_qt); tS.n += 1
                qt_t = tS.n

                def p3_qte(v, bnk=bnk, qt_t=qt_t):
                    v.wait_ge(tS.h, qt_t)
                    v.tensor_copy(qrm[:, 512 * bnk:512 * (bnk + 1)],
                                  tgtPS[:, :]).then_inc(vS.h, 1)
                VE(p3_qte); vS.n += 1
                p3s["qrm_v"] = vS.n
            qrm_v = vS.n

            def p3_st1(v, qh_all=qh_all):
                v.wait_ge(tS.h, qh_all)
                for bnk in range(4):
                    i1 = v.reduce_sum(stc[:, 4 * bnk:4 * bnk + 4],
                                      S_ps[bnk][:, :].rearrange("p (g e) -> p g e", g=4), axis=AX)
                fence(v, vS, i1)
                v.nop().then_inc(vS.h, 1)
            VE(p3_st1); vS.n += 1
            st1_v = vS.n

            def p3_sq(s, qh_all=qh_all):
                s.wait_ge(tS.h, qh_all)
                for bnk in range(4):
                    inst = s.activation(sqm[:, 512 * bnk:512 * (bnk + 1)],
                                        S_ps[bnk][:, :], AF.Square)
                inst.then_inc(aS.h, 1)
            SC(p3_sq); aS.n += 1
            sq_a = aS.n

            def p3_st2(v, sq_a=sq_a, st1_v=st1_v):
                v.wait_ge(aS.h, sq_a)
                v.wait_ge(vS.h, st1_v)
                i2 = v.reduce_sum(smallB[:, 16:32],
                                  sqm[:, :].rearrange("p (g e) -> p g e", g=16), axis=AX)
                fence(v, vS, i2)
                i3 = v.tensor_scalar(smallB[:, 0:16], stc[:, 0:16], 1.0 / 128.0, None, ALU.mult)
                i4 = v.tensor_scalar(smallB[:, 16:32], smallB[:, 16:32], 1.0 / 128.0, None, ALU.mult)
                fence(v, vS, i4)
                i5 = v.tensor_tensor(smallB[:, 32:48], smallB[:, 0:16], smallB[:, 0:16], ALU.mult)
                fence(v, vS, i5)
                v.tensor_tensor(smallB[:, 16:32], smallB[:, 16:32], smallB[:, 32:48],
                                ALU.subtract).then_inc(vS.h, 1)
            VE(p3_st2); vS.n += 1
            st2_v = vS.n

            def p3_sd(s, st2_v=st2_v):
                s.wait_ge(vS.h, st2_v)
                s.activation(smallB[:, 32:48], smallB[:, 16:32], AF.Sqrt,
                             bias=cbuf[:, 1:2]).then_inc(aS.h, 1)
            SC(p3_sd); aS.n += 1
            sd_a = aS.n

            def p3_ochain(v, sd_a=sd_a, qrm_v=qrm_v, ld=ld):
                v.wait_ge(aS.h, sd_a)
                v.wait_ge(vS.h, qrm_v)
                ir = v.reciprocal(smallB[:, 48:64], smallB[:, 32:48])
                fence(v, vS, ir)
                im = v.tensor_tensor(smallB[:, 0:16], smallB[:, 0:16], smallB[:, 48:64], ALU.mult)
                fence(v, vS, im)
                for bnk in range(4):
                    ob3 = obuf[:, 512 * bnk:512 * (bnk + 1)].rearrange("p (g e) -> p g e", g=4)
                    v.tensor_tensor(ob3, S_ps[bnk][:, :].rearrange("p (g e) -> p g e", g=4),
                                    bc3(smallB[:, 48 + 4 * bnk:52 + 4 * bnk]), ALU.mult)
                    iw = v.tensor_tensor(ob3, ob3, bc3(smallB[:, 4 * bnk:4 * bnk + 4]), ALU.subtract)
                fence(v, vS, iw)
                v.tensor_tensor(obuf[:, :], obuf[:, :], w_rm[:, :], ALU.mult)
                v.tensor_tensor(obuf[:, :], obuf[:, :], b_rm[:, :], ALU.add)
                v.tensor_tensor(obuf[:, :], obuf[:, :], qrm[:, :], ALU.add)
                iq = v.tensor_tensor(obuf[:, :], obuf[:, :], sgateb[:, :], ALU.mult)
                iq.then_inc(vS.h, 1)
            VE(p3_ochain); vS.n += 1
            oc_v = vS.n

            def p3_rsq(s, oc_v=oc_v):
                s.wait_ge(vS.h, oc_v)
                s.activation(sqm[:, :], obuf[:, :], AF.Square).then_inc(aS.h, 1)
            SC(p3_rsq); aS.n += 1
            rsq_a = aS.n

            def p3_rms1(v, rsq_a=rsq_a):
                v.wait_ge(aS.h, rsq_a)
                i1 = v.reduce_sum(smallB[:, 0:16],
                                  sqm[:, :].rearrange("p (g e) -> p g e", g=16), axis=AX)
                fence(v, vS, i1)
                i2 = v.tensor_scalar(smallB[:, 0:16], smallB[:, 0:16], 1.0 / 128.0, None, ALU.mult)
                fence(v, vS, i2)
                v.nop().then_inc(vS.h, 1)
            VE(p3_rms1); vS.n += 1
            rms1_v = vS.n

            def p3_rsd(s, rms1_v=rms1_v):
                s.wait_ge(vS.h, rms1_v)
                s.activation(smallB[:, 16:32], smallB[:, 0:16], AF.Sqrt,
                             bias=cbuf[:, 1:2]).then_inc(aS.h, 1)
            SC(p3_rsd); aS.n += 1
            rsd_a = aS.n

            def p3_fin(v, rsd_a=rsd_a):
                v.wait_ge(aS.h, rsd_a)
                ir = v.reciprocal(smallB[:, 16:32], smallB[:, 16:32])
                fence(v, vS, ir)
                ob3 = obuf[:, :].rearrange("p (g e) -> p g e", g=16)
                v.tensor_tensor(ob3, ob3, bc3(smallB[:, 16:32]), ALU.mult)
                v.tensor_tensor(opre[:, :], obuf[:, :], nw_rm[:, :],
                                ALU.mult).then_inc(vS.h, 1)
            VE(p3_fin); vS.n += 1
            fin_v = vS.n

            def p3_park(g, B=B, fin_v=fin_v):
                g.wait_ge(vS.h, fin_v)
                dma(g, d2, opre_dram[B, :, :], opre[:, :])
            GP(p3_park)
            p3s["opark"] = d2.n
        p3_done = dict(t=tS.n, v=vS.n, a=aS.n, d2=d2.n)

    if "p5" in BUILD_PARTS:
        if DEBUG2:
            dout("h_o2", [8, 128, 8192])
            dout("nhT_o2", [64, 128, 1024], BF16)
            dout("p1f_o2", [16, 128, 512])
            dout("gb_o2", [16, 128, 512], BF16)
        if DEBUG:
            dout("sg3_o", [4, 16, 128, 256])
            dout("gg3_o", [4, 16, 128, 256], BF16)
            dout("ot3_o", [4, 16, 128, 256])
            dout("tern_o", [2, 128, 8192], BF16)
            dout("ternodt_o", [128, 8192], BF16)
            dout("nhq_o", [128, 16384], BF16)
            dout("sxbc_o", [128, 256])
            dout("p1f_o", [8, 128, 512])
            dout("p2f_o", [8, 128, 512])
            dout("sxgc_o", [128, 32])
        pd = p3_done
        wfst = ARF[:, 0:8192]
        gbbcb = [ARF[:, 8192:8704], ARF[:, 11264:11776]]
        vbbcb = [ARF[:, 8704:9216], ARF[:, 11776:12288]]
        t1fb = [ARF[:, 9216:9728], ARF[:, 10240:10752]]
        t2fb = [ARF[:, 9728:10240], ARF[:, 10752:11264]]
        htile = ARF[:, 8192:16384]
        sgf = ARF[:, 8192:8448]
        sxbc = ARF[:, 8448:8704]
        xstage = [ARF[:, 9216:11264], ARF[:, 11264:13312]]
        otmp = ARF[:, 13312:13568]
        nxTrb = ARB[:, 0:16384]
        terng = ARB[:, 16384:24576]
        ternv = ARB[:, 24576:32768]
        gateb = ARB[:, 32768:33280]
        nh = ARB[:, 33280:41472]
        nhT = ARB[:, 41472:43520]
        nhq = ARB[:, 0:16384]
        ternodt = [ARB[:, 16384:24576], ARB[:, 35328:43520]]
        opreTq = ARB[:, 24576:28672]
        wobuf = ARB[:, 28672:30720]
        gslice = ARB[:, 30720:30976]
        opst = ARB[:, 33280:35328]
        x3 = xmir_dram

        def p5_init(g):
            g.wait_ge(d2.h, pd["d2"])
            g.wait_ge(vS.h, max(gwinv_v, pd["v"]))
            g.wait_ge(aS.h, pd["a"])
            g.wait_ge(tS.h, pd["t"])
            with g.register("rp5a") as rhf:
                g.reg_load(rhf, meta_sb[0:1, 3:4])
                hsel = g.snap(rhf)
                dma(g, d2, nxTrb[:, :], nxT_dram[bass.ds(hsel, 1), :, :])
            with g.register("rp5b") as rhf2:
                g.reg_load(rhf2, meta_sb[0:1, 3:4])
                hsel2 = g.snap(rhf2)
                dma(g, d2, sxg_col[:, 0:16], sxg_dram[bass.ds(hsel2, 1), :, :])
        GP(p5_init)
        p5i_d = d2.n

        def p5_initv(v):
            v.wait_ge(d2.h, p5i_d)
            v.nop().then_inc(vS.h, 1)
        VE(p5_initv); vS.n += 1
        p5i_v = vS.n

        l1 = dict(tern_v=p5i_v, mm_t={}, evv={}, hv={}, hdma={})
        for hc in range(16):
            for mi, (wd, tern, invcol) in enumerate(
                    ((gwT_d, terng, gamw128[:, 3:4]), (vwT_d, ternv, smallA[:, 20:21]))):
                def l1_wl(g, hc=hc, wd=wd):
                    g.wait_ge(vS.h, l1["tern_v"])
                    dma(g, d3, wfst[:, :].rearrange("p (dt c) -> p dt c", dt=16),
                        wd[:, 512 * hc:512 * (hc + 1)].rearrange("(dt p) c -> p dt c", p=128))
                GP(l1_wl)
                wrdy = d3.n

                def l1_tern(v, tern=tern, invcol=invcol, wrdy=wrdy, hc=hc):
                    v.wait_ge(d3.h, wrdy)
                    if (hc - 1, "mm") in l1["mm_t"]:
                        v.wait_ge(tS.h, l1["mm_t"][(hc - 1, "mm")])
                    v.tensor_scalar(wfst[:, :], wfst[:, :], invcol, None, ALU.mult)
                    for s4 in range(4):
                        v.tensor_copy(ibuf, wfst[:, 2048 * s4:2048 * (s4 + 1)])
                        inst = v.tensor_scalar(tern[:, 2048 * s4:2048 * (s4 + 1)], ibuf,
                                               -1.0, 1.0, ALU.max, ALU.min)
                    inst.then_inc(vS.h, 1)
                VE(l1_tern); vS.n += 1
                l1["tern_v"] = vS.n
                if DEBUG and hc == 0:
                    def l1_terndbg(sy, mi=mi, tv=l1["tern_v"], tern=tern):
                        sy.wait_ge(vS.h, tv)
                        dma(sy, d1, dbg["tern_o"][mi, :, :], tern[:, :])
                        if mi == 0:
                            dma(sy, d1, dbg["sxgc_o"], sxg_col[:, :])
                    SY(l1_terndbg)
            tern_done = vS.n

            def l1_bias(sy, hc=hc):
                if (hc - 2, 7) in l1["evv"]:
                    sy.wait_ge(vS.h, l1["evv"][(hc - 2, 7)])
                else:
                    sy.wait_ge(vS.h, p5i_v)
                dma(sy, d1, gbbcb[hc % 2][:, :], gbbc_d[:, 512 * hc:512 * (hc + 1)])
                dma(sy, d1, vbbcb[hc % 2][:, :], vbbc_d[:, 512 * hc:512 * (hc + 1)])
            SY(l1_bias)
            bias_rdy = d1.n

            for ti in range(8):
                def l1_mm(te, hc=hc, ti=ti, tern_done=tern_done):
                    te.wait_ge(vS.h, tern_done)
                    te.wait_ge(d2.h, p5i_d)
                    prev = l1["evv"].get((hc, ti - 2)) or l1["evv"].get((hc - 1, 6 + ti))
                    if prev:
                        te.wait_ge(vS.h, prev)
                    pg, pv = (S_ps[0], S_ps[1]) if ti % 2 == 0 else (S_ps[2], S_ps[3])
                    for dt in range(16):
                        te.matmul(pg[:, :], nxTrb[:, 1024 * dt + 128 * ti:1024 * dt + 128 * (ti + 1)],
                                  terng[:, 512 * dt:512 * (dt + 1)],
                                  start=(dt == 0), stop=(dt == 15))
                    for dt in range(16):
                        inst = te.matmul(pv[:, :], nxTrb[:, 1024 * dt + 128 * ti:1024 * dt + 128 * (ti + 1)],
                                         ternv[:, 512 * dt:512 * (dt + 1)],
                                         start=(dt == 0), stop=(dt == 15))
                    inst.then_inc(tS.h, 1)
                TE(l1_mm); tS.n += 1
                l1["mm_t"][(hc, "mm")] = tS.n
                mmt = tS.n

                def l1_ev(v, hc=hc, ti=ti, mmt=mmt, bias_rdy=bias_rdy):
                    v.wait_ge(tS.h, mmt)
                    v.wait_ge(d1.h, bias_rdy)
                    gi = (8 * hc + ti) % 2
                    prevd = l1["hdma"].get((hc, ti - 2)) or l1["hdma"].get((hc - 1, 6 + ti))
                    if prevd:
                        v.wait_ge(d2.h, prevd)
                    t1f, t2f = t1fb[gi], t2fb[gi]
                    pg, pv = (S_ps[0], S_ps[1]) if ti % 2 == 0 else (S_ps[2], S_ps[3])
                    v.tensor_scalar(t1f[:, :], pg[:, :], sxg_col[:, ti:ti + 1],
                                    None, ALU.mult)
                    v.tensor_tensor(t1f[:, :], t1f[:, :], gbbcb[hc % 2][:, :], ALU.add)
                    v.tensor_scalar(t2f[:, :], pv[:, :], sxg_col[:, 8 + ti:9 + ti],
                                    None, ALU.mult)
                    v.tensor_tensor(t2f[:, :], t2f[:, :], vbbcb[hc % 2][:, :],
                                    ALU.add).then_inc(vS.h, 1)
                VE(l1_ev); vS.n += 1
                l1["evv"][(hc, ti)] = vS.n
                evv = vS.n
                if DEBUG and hc == 0:
                    def l1_evdbg(sy, ti=ti, evv=evv, hc=hc):
                        sy.wait_ge(vS.h, evv)
                        gi = (8 * hc + ti) % 2
                        dma(sy, d1, dbg["p1f_o"][ti, :, :], t1fb[gi][:, :])
                        dma(sy, d1, dbg["p2f_o"][ti, :, :], t2fb[gi][:, :])
                    SY(l1_evdbg)
                    l1["evdbg"] = d1.n

                def l1_silu(sc, evv=evv, hc=hc, ti=ti):
                    sc.wait_ge(vS.h, evv)
                    if DEBUG and hc == 0:
                        sc.wait_ge(d1.h, l1["evdbg"])
                    sc.activation(gateb[:, :], t1fb[(8 * hc + ti) % 2][:, :],
                                  AF.Silu).then_inc(aS.h, 1)
                SC(l1_silu); aS.n += 1
                sla = aS.n

                def l1_h(v, sla=sla, hc=hc, ti=ti):
                    v.wait_ge(aS.h, sla)
                    gi = (8 * hc + ti) % 2
                    v.tensor_tensor(t1fb[gi][:, :], gateb[:, :], t2fb[gi][:, :],
                                    ALU.mult).then_inc(vS.h, 1)
                VE(l1_h); vS.n += 1
                l1["hv"][(hc, ti)] = vS.n
                hv = vS.n
                if DEBUG2 and ti == 0:
                    def l1_ff(sy, hc=hc, evv=evv, sla=sla):
                        sy.wait_ge(vS.h, evv)
                        dma(sy, d1, dbg["p1f_o2"][hc, :, :], t1fb[(8 * hc) % 2][:, :])
                        sy.wait_ge(aS.h, sla)
                        dma(sy, d1, dbg["gb_o2"][hc, :, :], gateb[:, :])
                    SY(l1_ff)

                def l1_hd(g, hc=hc, ti=ti, hv=hv):
                    g.wait_ge(vS.h, hv)
                    dma(g, d2, h_dram[ti, :, 512 * hc:512 * (hc + 1)],
                        t1fb[(8 * hc + ti) % 2][:, :])
                GP(l1_hd)
                l1["hdma"][(hc, ti)] = d2.n
        l1_done = dict(v=vS.n, t=tS.n, d2=d2.n)

        l2t = dict(v=l1["tern_v"])
        for ht in range(64):
            def o_wl(sy, ht=ht):
                sy.wait_ge(vS.h, l2t["v"])
                if ht == 0:
                    sy.wait_ge(vS.h, l1_done["v"])
                dma(sy, d3, wfst[:, 0:2048], owT_d[128 * ht:128 * (ht + 1), :])
            SC(o_wl)
            wrdy = d3.n

            def o_t1(v, wrdy=wrdy, ht=ht):
                v.wait_ge(d3.h, wrdy)
                if ht == 0:
                    v.wait_ge(vS.h, l1_done["v"])
                if ("otd", ht - 1) in l2t:
                    v.wait_ge(d2.h, l2t[("otd", ht - 1)])
                v.tensor_scalar(wfst[:, 0:2048], wfst[:, 0:2048], smallA[:, 22:23],
                                None, ALU.mult)
                v.tensor_copy(ibuf, wfst[:, 0:2048])
                v.tensor_scalar(ternv[:, 0:2048], ibuf, -1.0, 1.0,
                                ALU.max, ALU.min).then_inc(vS.h, 1)
            VE(o_t1); vS.n += 1
            l2t["v"] = vS.n
            t3v = vS.n

            def o_td(g, ht=ht, t3v=t3v):
                g.wait_ge(vS.h, t3v)
                outap = terno_dram[:, ht, :, :].rearrange("dt p c -> p dt c")
                dma(g, d2, outap, ternv[:, 0:2048].rearrange("p (dt c) -> p dt c", dt=16))
            GP(o_td)
            l2t[("otd", ht)] = d2.n
        terno_done = d2.n

        l2 = dict(nhv={}, ev={}, qd={}, last_t=0, sxe_a=0)
        for ti in range(8):
            def l2_hl(g, ti=ti):
                g.wait_ge(d2.h, l1_done["d2"])
                if (ti - 1) in l2["nhv"]:
                    g.wait_ge(vS.h, l2["nhv"][ti - 1])
                dma(g, d2, htile[:, :], h_dram[ti, :, :])
            GP(l2_hl)
            hrdy = d2.n

            def l2_g(v, hrdy=hrdy, ti=ti):
                v.wait_ge(d2.h, hrdy)
                if l2["last_t"]:
                    v.wait_ge(tS.h, l2["last_t"])
                i1 = v.reduce_max(smallB[:, 0:1], htile[:, :], axis=AX,
                                  apply_absolute_value=True)
                fence(v, vS, i1)
                i2 = v.tensor_scalar(smallB[:, 1:2], smallB[:, 0:1], 1e-5, None, ALU.max)
                fence(v, vS, i2)
                i3 = v.tensor_scalar(smallB[:, 2:3], smallB[:, 1:2], 1.0 / 127.0, None, ALU.mult)
                fence(v, vS, i3)
                ir = v.reciprocal(smallB[:, 3:4], smallB[:, 2:3])
                fence(v, vS, ir)
                i4 = v.tensor_tensor(smallB[:, 4:5], smallB[:, 2:3], gamw128[:, 2:3], ALU.mult)
                fence(v, vS, i4)
                v.tensor_scalar(htile[:, :], htile[:, :], smallB[:, 3:4], None, ALU.mult)
                for s4 in range(4):
                    v.tensor_copy(ibuf, htile[:, 2048 * s4:2048 * (s4 + 1)])
                    inst = v.tensor_copy(nh[:, 2048 * s4:2048 * (s4 + 1)], ibuf)
                inst.then_inc(vS.h, 1)
            VE(l2_g); vS.n += 1
            l2["nhv"][ti] = vS.n
            nhv = vS.n

            def l2_sxt(te, ti=ti, nhv=nhv):
                te.wait_ge(vS.h, nhv)
                if l2["sxe_a"]:
                    te.wait_ge(aS.h, l2["sxe_a"])
                te.matmul(khPS[0:1, 0:128], smallB[:, 4:5], identf[:, :],
                          start=True, stop=True).then_inc(tS.h, 1)
            TE(l2_sxt); tS.n += 1
            sxt = tS.n

            def l2_sxe(s, ti=ti, sxt=sxt):
                s.wait_ge(tS.h, sxt)
                s.copy(sxhrowF[0:1, 128 * ti:128 * (ti + 1)],
                       khPS[0:1, 0:128]).then_inc(aS.h, 1)
            SC(l2_sxe); aS.n += 1
            l2["sxe_a"] = aS.n

            for grp in range(16):
                def l2_t(te, grp=grp, nhv=nhv, ti=ti):
                    te.wait_ge(vS.h, nhv)
                    prev = l2["ev"].get((ti, grp - 1)) or l2["ev"].get((ti - 1, 15))
                    if prev:
                        te.wait_ge(vS.h, prev)
                    for i in range(4):
                        ht = 4 * grp + i
                        inst = te.transpose(tgtPS[:, 128 * i:128 * (i + 1)],
                                            nh[:, 128 * ht:128 * (ht + 1)], identb[:, :])
                    inst.then_inc(tS.h, 1)
                TE(l2_t); tS.n += 1
                l2["last_t"] = tS.n
                tt = tS.n

                def l2_te(v, grp=grp, tt=tt, ti=ti):
                    v.wait_ge(tS.h, tt)
                    if grp % 4 == 0:
                        prevq = l2["qd"].get((ti, grp // 4 - 1)) or l2["qd"].get((ti - 1, 3))
                        if prevq:
                            v.wait_ge(d2.h, prevq)
                    v.tensor_copy(nhT[:, 512 * (grp % 4):512 * (grp % 4 + 1)],
                                  tgtPS[:, :]).then_inc(vS.h, 1)
                VE(l2_te); vS.n += 1
                l2["ev"][(ti, grp)] = vS.n
                if grp % 4 == 3:
                    tev = vS.n

                    def l2_nhd(g, grp=grp, ti=ti, tev=tev):
                        g.wait_ge(vS.h, tev)
                        h0 = 16 * (grp // 4)
                        outap = nhT_dram[h0:h0 + 16, :, 128 * ti:128 * (ti + 1)].rearrange("ht p c -> p ht c")
                        dma(g, d2, outap, nhT[:, 0:2048].rearrange("p (ht c) -> p ht c", ht=16))
                    GP(l2_nhd)
                    l2["qd"][(ti, grp // 4)] = d2.n
        l2_done = dict(v=vS.n, t=tS.n, d2=d2.n, a=aS.n)

        wo3 = wo_d.rearrange("(dt p) c -> p dt c", p=128)
        l3 = dict(res={}, oTe_v=0, oT_t=0, mm3_t=0, ev3_v=0, acc_v=0, xd={})
        for tq in range(4):
            def l3_ld(g, tq=tq):
                g.wait_ge(d2.h, max(terno_done, l2_done["d2"]))
                if tq > 0:
                    g.wait_ge(vS.h, l3["res"][tq - 1])
                    g.wait_ge(d2.h, l3["xd"][tq - 1])
                g.wait_ge(d3.h, xmir_d)
                with g.register(f"rx{tq}") as rx:
                    g.reg_load(rx, meta_sb[0:1, 40 + tq:41 + tq])
                    xq0 = g.snap(rx)
                    dma(g, d2, ARF[:, 9216:13312], x3[bass.ds(xq0, 1), :, :])
                inap = nhT_dram[:, :, 256 * tq:256 * (tq + 1)].rearrange("ht p c -> p ht c")
                dma(g, d2, nhq[:, :].rearrange("p (ht c) -> p ht c", ht=64), inap)
            GP(l3_ld)
            ld3 = d2.n

            for b2 in range(2):
                def l3_og(g, tq=tq, b2=b2, ld3=ld3):
                    g.wait_ge(d2.h, max(ld3, terno_done))
                    if l3["oTe_v"]:
                        g.wait_ge(vS.h, l3["oTe_v"])
                    dma(g, d2, opst[:, :], opre_dram[2 * tq + b2, :, :])
                GP(l3_og)
                og_d = d2.n
                for grp in range(4):
                    def l3_oT(te, grp=grp, og_d=og_d):
                        te.wait_ge(d2.h, og_d)
                        if l3["oTe_v"]:
                            te.wait_ge(vS.h, l3["oTe_v"])
                        for i in range(4):
                            h = 4 * grp + i
                            inst = te.transpose(krmPS[:, 128 * i:128 * (i + 1)],
                                                opst[:, 128 * h:128 * (h + 1)],
                                                identb[:, :])
                        inst.then_inc(tS.h, 1)
                    TE(l3_oT); tS.n += 1
                    l3["oT_t"] = tS.n
                    ott = tS.n

                    def l3_oTe(v, b2=b2, grp=grp, ott=ott):
                        v.wait_ge(tS.h, ott)
                        outap = opreTq[:, :].rearrange("p (dt c) -> p dt c", dt=16)[:, 4 * grp:4 * grp + 4, 128 * b2:128 * (b2 + 1)]
                        v.tensor_copy(outap, krmPS[:, :].rearrange("p (i c) -> p i c", i=4)).then_inc(vS.h, 1)
                    VE(l3_oTe); vS.n += 1
                    l3["oTe_v"] = vS.n
            oT_done_v = vS.n

            def l3_sx(te, tq=tq, oT_done_v=oT_done_v):
                te.wait_ge(aS.h, l2_done["a"])
                te.wait_ge(vS.h, max(oT_done_v, p5i_v))
                te.matmul(ps_s[:, 0:256], onerow[0:1, :],
                          sxhrowF[0:1, 256 * tq:256 * (tq + 1)],
                          start=True, stop=True).then_inc(tS.h, 1)
            TE(l3_sx); tS.n += 1
            sxt3 = tS.n

            def l3_sxe(v, sxt3=sxt3):
                v.wait_ge(tS.h, sxt3)
                v.tensor_copy(sxbc[:, :], ps_s[:, 0:256]).then_inc(vS.h, 1)
            VE(l3_sxe); vS.n += 1
            sxe3 = vS.n

            for dtout in range(16):
                def l3_tl(g, tq=tq, dtout=dtout):
                    if DEBUG and l3.get("mmdbg"):
                        g.wait_ge(d1.h, l3["mmdbg"])
                    gi = dtout % 2
                    prev = l3.get(("mm3", tq, dtout - 2)) or l3.get(("mm3", tq - 1, 14 + dtout))
                    if prev:
                        g.wait_ge(tS.h, prev)
                    elif not l3["mm3_t"]:
                        g.wait_ge(d2.h, l2_done["d2"])
                    else:
                        g.wait_ge(tS.h, l3["mm3_t"])
                    dma(g, d3, ternodt[gi][:, :].rearrange("p (ht c) -> p ht c", ht=64),
                        terno_dram[dtout, :, :, :].rearrange("ht p c -> p ht c"))
                GP(l3_tl)
                tl3 = d3.n

                def l3_wol(g, tq=tq, dtout=dtout):
                    if l3["mm3_t"]:
                        g.wait_ge(tS.h, l3["mm3_t"])
                    dma(g, d2, wobuf[:, :].rearrange("p (dt c) -> p dt c", dt=16),
                        wo3[:, :, 128 * dtout:128 * (dtout + 1)])
                GP(l3_wol)
                wol3 = d2.n

                def l3_mm(te, tl3=tl3, wol3=wol3, ld3=ld3, oT_done_v=oT_done_v, sxe3=sxe3, dtout=dtout):
                    te.wait_ge(d3.h, tl3)
                    te.wait_ge(d2.h, max(wol3, ld3))
                    te.wait_ge(vS.h, max(oT_done_v, sxe3))
                    preva = l3.get(("acc2", tq, dtout - 2)) or l3.get(("acc2", tq - 1, 14 + dtout))
                    if preva:
                        te.wait_ge(vS.h, preva)
                    po, pw = (S_ps[0], S_ps[1]) if dtout % 2 == 0 else (S_ps[2], S_ps[3])
                    for ht in range(64):
                        te.matmul(po[:, 0:256], ternodt[dtout % 2][:, 128 * ht:128 * (ht + 1)],
                                  nhq[:, 256 * ht:256 * (ht + 1)],
                                  start=(ht == 0), stop=(ht == 63))
                    for dt in range(16):
                        inst = te.matmul(pw[:, 0:256], wobuf[:, 128 * dt:128 * (dt + 1)],
                                         opreTq[:, 256 * dt:256 * (dt + 1)],
                                         start=(dt == 0), stop=(dt == 15))
                    inst.then_inc(tS.h, 1)
                TE(l3_mm); tS.n += 1
                l3["mm3_t"] = tS.n
                l3[("mm3", tq, dtout)] = tS.n
                mm3 = tS.n
                if DEBUG and tq == 0 and dtout == 0:
                    def l3_mmdbg(sy, mm3=mm3):
                        sy.wait_ge(tS.h, mm3)
                        dma(sy, d1, dbg["ternodt_o"], ternodt[0][:, :])
                        dma(sy, d1, dbg["nhq_o"], nhq[:, :])
                        dma(sy, d1, dbg["sxbc_o"], sxbc[:, :])
                    SY(l3_mmdbg)
                    l3["mmdbg"] = d1.n

                def l3_g(v, mm3=mm3, dtout=dtout):
                    v.wait_ge(tS.h, mm3)
                    if DEBUG and l3.get("dbg_d"):
                        v.wait_ge(d1.h, l3["dbg_d"])
                    po = S_ps[0] if dtout % 2 == 0 else S_ps[2]
                    v.tensor_tensor(otmp[:, :], po[:, 0:256], sxbc[:, :],
                                    ALU.mult)
                    v.tensor_scalar(otmp[:, :], otmp[:, :], ob_sb[:, dtout:dtout + 1],
                                    None, ALU.add).then_inc(vS.h, 1)
                VE(l3_g); vS.n += 1
                gv = vS.n

                def l3_silu(sc, gv=gv):
                    sc.wait_ge(vS.h, gv)
                    sc.activation(gslice[:, :], otmp[:, :],
                                  AF.Silu).then_inc(aS.h, 1)
                SC(l3_silu); aS.n += 1
                sil3 = aS.n

                def l3_sg(v, sil3=sil3, dtout=dtout):
                    v.wait_ge(aS.h, sil3)
                    pw = S_ps[1] if dtout % 2 == 0 else S_ps[3]
                    v.tensor_tensor(sgf[:, :], pw[:, 0:256], gslice[:, :],
                                    ALU.mult).then_inc(vS.h, 1)
                VE(l3_sg); vS.n += 1
                sgv = vS.n

                if DEBUG:
                    def l3_dbg(sy, tq=tq, dtout=dtout, sgv=sgv):
                        sy.wait_ge(vS.h, sgv)
                        dma(sy, d1, dbg["sg3_o"][tq, dtout, :, :], sgf[:, :])
                        dma(sy, d1, dbg["gg3_o"][tq, dtout, :, :], gslice[:, :])
                        dma(sy, d1, dbg["ot3_o"][tq, dtout, :, :], otmp[:, :])
                    SY(l3_dbg)
                    l3["dbg_d"] = d1.n

                for g2 in range(2):
                    def l3_T(te, g2=g2, sgv=sgv, ld3=ld3):
                        te.wait_ge(vS.h, sgv)
                        te.wait_ge(d2.h, ld3)
                        if l3["acc_v"]:
                            te.wait_ge(vS.h, l3["acc_v"])
                        te.matmul(khPS[:, 0:128], sgf[:, 128 * g2:128 * (g2 + 1)],
                                  identf[:, :], start=True, stop=True).then_inc(tS.h, 1)
                    TE(l3_T); tS.n += 1
                    t3 = tS.n

                    def l3_acc(v, g2=g2, dtout=dtout, t3=t3, tq=tq):
                        v.wait_ge(tS.h, t3)
                        if dtout == 0 and l3["xd"].get(tq - 1):
                            v.wait_ge(d2.h, l3["xd"][tq - 1])
                        v.tensor_tensor(xstage[g2][:, 128 * dtout:128 * (dtout + 1)],
                                        xstage[g2][:, 128 * dtout:128 * (dtout + 1)],
                                        khPS[:, 0:128], ALU.add).then_inc(vS.h, 1)
                    VE(l3_acc); vS.n += 1
                    l3["acc_v"] = vS.n
                l3["ev3_v"] = vS.n
                l3[("acc2", tq, dtout)] = vS.n
            res_v = vS.n
            l3["res"][tq] = res_v

            def l3_out(g, tq=tq, res_v=res_v):
                g.wait_ge(vS.h, res_v)
                dma(g, d2, out_d[256 * tq:256 * tq + 128, :], xstage[0][:, :])
                dma(g, d2, out_d[256 * tq + 128:256 * tq + 256, :], xstage[1][:, :])
            GP(l3_out)
            l3["xd"][tq] = d2.n
        p5_done = dict(t=tS.n, v=vS.n, a=aS.n, d2=d2.n)

    if DEBUG:
        parcel_o = dout("parcel_o", [16, 128, 12288], BF16)
        q_o = dout("q_o", [128, H * T], BF16)
        etaG_o = dout("etaG_o", [128, 512])
        betaT_o = dout("betaT_o", [H, T])
        gamw_o = dout("gamw_o", [1, 8])
        sxg_o = dout("sxg_o", [128, 32])
        nxT_o = dout("nxT_o", [128, 16 * T], BF16)
        sg_o = dout("sg_o", [16, 128, D], BF16)
        xnT_o = dout("xnT_o", [128, 16 * TR], BF16)
        nxrm_o = dout("nxrm_o", [128, D], BF16)
        gxsx_o = dout("gxsx_o", [128, 32])
        smallA_o = dout("smallA_o", [128, 64])
        opre_o = dout("opre_o", [8, 128, D], BF16)
        h_o = dout("h_o", [8, 128, 8192])
        knG_o = dout("knG_o", [128, 128, 512], BF16)
        vstage_o = dout("vstage_o", [128, 2048], BF16)
        rmst_o = dout("rmst_o", [128, 1024], BF16)
        vG_o = dout("vG_o", [128, 128, 512], BF16)
        nhT_o = dout("nhT_o", [64, 128, 1024], BF16)
        sxh_o = dout("sxh_o", [1, 1024])

        def dbg_dump(sy):
            sy.wait_ge(vS.h, vS.n)
            sy.wait_ge(aS.h, aS.n)
            sy.wait_ge(tS.h, tS.n)
            sy.wait_ge(d2.h, d2.n)
            dma(sy, d1, etaG_o, etaG[:, :])
            dma(sy, d1, betaT_o, betaT[:, :])
            dma(sy, d1, gamw_o, rowA[0:1, 0:8])
            dma(sy, d1, sxg_o, sxg_col[:, :])
            dma(sy, d1, xnT_o, xnT[:, :])
            dma(sy, d1, nxrm_o, nxrm[:, :])
            dma(sy, d1, gxsx_o[:, 0:16], gx_col[:, :])
            dma(sy, d1, gxsx_o[:, 16:32], sx_col[:, :])
            dma(sy, d1, smallA_o, smallA[:, :])
        SY(dbg_dump)

        def dbg_dump2(g):
            g.wait_ge(vS.h, vS.n)
            g.wait_ge(d2.h, d2.n - 16)
            dma(g, d2, nxT_o, nxT_dram[:, :])
            dma(g, d2, knG_o, knG_dram[:, :, :])
            dma(g, d2, vstage_o, vstage[:, :])
            dma(g, d2, rmst_o, rmstage[1][:, :])
            dma(g, d2, vG_o, vG_dram[:, :, :])
            if "p5" in BUILD_PARTS:
                dma(g, d2, h_o, h_dram[:, :, :])
                dma(g, d2, nhT_o, nhT_dram[:, :, :])
                dma(g, d2, sxh_o, sxhrowF[0:1, :])
            dma(g, d2, parcel_o, parcel_dram[:, :, :])
            if "p3" in BUILD_PARTS:
                dma(g, d2, opre_o, opre_dram[:, :, :])
        GP(dbg_dump2)

    if DEBUG2 and "p5" in BUILD_PARTS:
        def dbg2_dump(g):
            g.wait_ge(vS.h, vS.n)
            g.wait_ge(d2.h, d2.n)
            dma(g, d2, dbg["h_o2"], h_dram[:, :, :])
            dma(g, d2, dbg["nhT_o2"], nhT_dram[:, :, :])
        GP(dbg2_dump)

    outx = nc.dram_tensor("outx", [1, 4], F32, kind="ExternalOutput").ap()

    def fin(sy):
        sy.wait_ge(d1.h, d1.n)
        sy.wait_ge(d2.h, d2.n)
        sy.wait_ge(d3.h, d3.n)
        dma(sy, d1, outx, rowA[0:1, 0:4])
        sy.wait_ge(d1.h, d1.n)
    SY(fin)

    st.close()
    return nc


def host_prep(inputs):
    ln_w = np.asarray(inputs["L0_attn_ln_w"], np.float32)
    ln_b = np.asarray(inputs["L0_attn_ln_b"], np.float32)
    lnwG = np.zeros((128, 512), np.float32)
    lnbG = np.zeros((128, 512), np.float32)
    for ph in range(4):
        for g in range(4):
            h = 4 * ph + g
            lnwG[32 * ph:32 * ph + 16, 128 * g:128 * (g + 1)] = ln_w[h][None, :]
            lnbG[32 * ph:32 * ph + 16, 128 * g:128 * (g + 1)] = ln_b[h][None, :]
    lnwT = np.ascontiguousarray(ln_w.T)
    ident = np.eye(128, dtype=np.float32)
    kt = np.arange(128)[:, None]; qt = np.arange(128)[None, :]
    maskT = (((kt // 16) < (qt // 16)) | (((kt // 16) == (qt // 16)) & (kt <= qt))).astype(np.float32)
    return dict(
        wq=np.ascontiguousarray(inputs["L0_attn_Wq"], np.float32),
        wk=np.ascontiguousarray(inputs["L0_attn_Wk"], np.float32),
        wv=np.ascontiguousarray(inputs["L0_attn_Wv"], np.float32),
        wo=np.ascontiguousarray(inputs["L0_attn_Wo"], np.float32),
        wbeta=np.ascontiguousarray(inputs["L0_attn_Wbeta"], np.float32),
        wg1=np.ascontiguousarray(inputs["L0_attn_Wg1"], np.float32),
        wg2=np.ascontiguousarray(inputs["L0_attn_Wg2"], np.float32),
        lnwG=lnwG, lnbG=lnbG, lnwT=lnwT, ident=ident, maskT=maskT,
        wrm=np.tile(ln_w.reshape(1, 2048), (128, 1)).astype(np.float32),
        brm=np.tile(ln_b.reshape(1, 2048), (128, 1)).astype(np.float32),
        nwrm=np.tile(np.asarray(inputs["L0_attn_norm_w"], np.float32).reshape(1, 2048), (128, 1)),
        gwT=np.ascontiguousarray(np.asarray(inputs["L0_ffn_gate_w"], np.float32).T),
        vwT=np.ascontiguousarray(np.asarray(inputs["L0_ffn_val_w"], np.float32).T),
        owT=np.ascontiguousarray(np.asarray(inputs["L0_ffn_out_w"], np.float32).T),
        gbbc=np.ascontiguousarray(np.tile(np.asarray(inputs["L0_ffn_gate_b"], np.float32)[None, :], (128, 1))),
        vbbc=np.ascontiguousarray(np.tile(np.asarray(inputs["L0_ffn_val_b"], np.float32)[None, :], (128, 1))),
        obm=np.ascontiguousarray(np.asarray(inputs["L0_ffn_out_b"], np.float32).reshape(16, 128).T),
    )


_prog_cache = {}


def get_program():
    if "nc" not in _prog_cache:
        _prog_cache["nc"] = build_program()
    return _prog_cache["nc"]


def make_meta(half):
    m = np.zeros((1, 64), np.int32)
    m[0, 0] = 1024 * half
    m[0, 1] = 8 * half
    for B in range(8):
        m[0, 8 + B] = 8 * half + B
        m[0, 16 + B] = 128 * (8 * half + B)
        m[0, 24 + B] = 8 * (8 * half + B)
    m[0, 3] = half
    for tq in range(4):
        m[0, 32 + tq] = 1024 * half + 256 * tq
        m[0, 36 + tq] = 1024 * half + 256 * tq + 128
        m[0, 40 + tq] = 4 * half + tq
    return m


def kernel(**inputs):
    x = np.asarray(inputs["L0_attn_x"], np.float32)
    base = host_prep(inputs)
    nc = get_program()
    in_maps = []
    for c in range(8):
        b, half = c // 2, c % 2
        m = dict(base)
        m["x"] = np.ascontiguousarray(x[b])
        m["meta"] = make_meta(half)
        in_maps.append(m)
    res = run_bass_kernel_spmd(nc, in_maps, core_ids=list(range(8)))
    out = np.empty((4, 2048, 2048), np.float32)
    for c in range(8):
        b, half = c // 2, c % 2
        out[b, 1024 * half:1024 * (half + 1), :] = np.asarray(res.results[c]["out"], np.float32)
    return out
